# version 11
# baseline (speedup 1.0000x reference)
import numpy as np
import ml_dtypes
import concourse.bass as bass
import concourse.mybir as mybir
from concourse.bass_utils import run_bass_kernel_spmd
from contextlib import ExitStack

F32 = mybir.dt.float32
BF16 = mybir.dt.bfloat16
AF = mybir.ActivationFunctionType
ALU = mybir.AluOpType
AX = mybir.AxisListType

P = 128
B_FULL, T, DM = 16, 4096, 1024
NCORES = 8
SEQ_PER_CORE = B_FULL // NCORES
TOK = SEQ_PER_CORE * T
NCH_SEQ = T // P
NCH = TOK // P
DI = 2816
HI = DI // 2
NKC = DM // P
NKI = DI // P
NHP = 8

MID2S = [(m * m) / 36.0 for m in (0.25, 0.75, 1.25, 1.75, 2.5, 3.5, 5.0)]

ENGS = ("sync", "vector", "scalar", "tensor", "gpsimd")
ENG_SEM = {"sync": "s_dma", "vector": "s_v", "scalar": "s_a",
           "tensor": "s_pe", "gpsimd": "s_g"}
NDMASEM = 8
PREP_ON = True
A_ON = True
B_ON = True
A_PREP = True
A_QKV = True
A_TR = True
A_ATT = True
A_WO = True
A_ATT_LVL = 6


class Buf:
    __slots__ = ("w", "r")

    def __init__(self):
        self.w = None
        self.r = []


class Prog:
    def __init__(self, nc):
        self.nc = nc
        self.steps = {e: [] for e in ENGS}
        self.sems = {}
        self.cnt = {}
        self.dma_idx = 0

    def sem(self, name, handle):
        self.sems[name] = handle
        self.cnt[name] = 0

    def step(self, eng, emit, reads=(), writes=(), extra=(), dma=False):
        deps = []
        for b in reads:
            if b.w is not None:
                deps.append(b.w)
        for b in writes:
            if b.w is not None:
                deps.append(b.w)
            deps.extend(b.r)
        deps.extend(extra)
        if dma:
            idx = self.dma_idx
            self.dma_idx += 1
            semn = "s_d%d" % (idx % NDMASEM)
            self.cnt[semn] += 16
            if self.cnt[semn] > 16:
                deps.append((eng, semn, self.cnt[semn] - 16, True))
            tok = (eng, semn, self.cnt[semn], False)
            self.steps[eng].append((tuple(deps), emit, semn, 16))
        else:
            semn = ENG_SEM[eng]
            self.cnt[semn] += 1
            tok = (eng, semn, self.cnt[semn], False)
            self.steps[eng].append((tuple(deps), emit, semn, 1))
        for b in reads:
            b.r.append(tok)
        for b in writes:
            b.w = tok
            b.r = []
        return tok

    def flush(self, block):
        if not hasattr(self, "waited"):
            self.waited = {e: dict.fromkeys(self.sems, 0) for e in ENGS}

        def mk(eng_name, steps):
            def body(eng):
                w = self.waited[eng_name]
                for deps, emit, semn, inc in steps:
                    for (peng, psem, pcnt, psync) in deps:
                        if w[psem] >= pcnt:
                            continue
                        eng.wait_ge(self.sems[psem], pcnt)
                        w[psem] = pcnt
                    inst = emit(eng)
                    inst.then_inc(self.sems[semn], inc)
            return body

        for e in ENGS:
            if self.steps[e]:
                getattr(block, e)(mk(e, self.steps[e]))
        self.steps = {e: [] for e in ENGS}


def r3(t, k=16):
    return t.rearrange("p (b k) -> p b k", k=k)


def bc16(t, nb):
    return t[:, 0:nb].unsqueeze(2).broadcast_to([P, nb, 16])


class Quant:

    _uid = [0]

    def __init__(self, nc, sb, qw, cst_t, cst_b):
        self.qw = qw
        self.cst_t = cst_t
        self.cst_b = cst_b
        nbw = qw // 16
        u = "%d" % Quant._uid[0]
        Quant._uid[0] += 1
        self.sg = sb("q_sg" + u, [P, qw], BF16)
        self.t = [sb("q_t%d_" % i + u, [P, qw], BF16) for i in range(6)]
        self.bm2 = sb("q_bm2" + u, [P, nbw], F32)
        self.iv2 = sb("q_iv2" + u, [P, nbw], F32)
        self.shh = sb("q_sh" + u, [P, nbw], BF16)
        self.b_sg = Buf()
        self.b_t = [Buf() for _ in range(6)]
        self.b_bm2 = Buf()
        self.b_iv2 = Buf()
        self.b_sh = Buf()

    def emit_multi(self, p, src_t, src_b, dst_t, dst_b, width, src_off=0,
                   dst_off=0, z2_eng="vector", dst_eng="gpsimd"):
        off = 0
        while off < width:
            w = min(self.qw, width - off)
            self.emit(p, src_t, src_b, dst_t, dst_b, w, src_off + off,
                      dst_off + off, z2_eng, dst_eng)
            off += w


    def emit(self, p, src_t, src_b, dst_t, dst_b, width, src_off=0, dst_off=0,
             z2_eng="vector", dst_eng="gpsimd"):
        nb = width // 16
        sg, t = self.sg, self.t
        bsg, bt = self.b_sg, self.b_t
        x2 = src_t[:, src_off:src_off + width]
        bx2 = src_b
        p.step("vector", lambda e: e.tensor_reduce(out=self.bm2[:, 0:nb],
               in_=r3(x2), axis=AX.X, op=ALU.max, apply_absolute_value=True),
               reads=[src_b], writes=[self.b_bm2])
        p.step("scalar", lambda e: e.activation(out=sg[:, 0:width], in_=x2,
                                                func=AF.Sign),
               reads=[src_b], writes=[bsg])
        p.step("scalar", lambda e: e.activation(out=x2, in_=x2, func=AF.Square),
               reads=[src_b], writes=[bx2])
        p.step("vector", lambda e: e.tensor_scalar(out=self.bm2[:, 0:nb],
               in0=self.bm2[:, 0:nb], scalar1=6e-12, scalar2=None, op0=ALU.max),
               reads=[self.b_bm2], writes=[self.b_bm2])
        p.step("vector", lambda e: e.tensor_scalar(out=self.shh[:, 0:nb],
               in0=self.bm2[:, 0:nb], scalar1=1.0 / 12.0, scalar2=None,
               op0=ALU.mult),
               reads=[self.b_bm2], writes=[self.b_sh])
        p.step("vector", lambda e: e.reciprocal(self.iv2[:, 0:nb], self.bm2[:, 0:nb]),
               reads=[self.b_bm2], writes=[self.b_iv2])
        p.step("vector", lambda e: e.tensor_tensor(out=self.iv2[:, 0:nb],
               in0=self.iv2[:, 0:nb], in1=self.iv2[:, 0:nb], op=ALU.mult),
               reads=[self.b_iv2], writes=[self.b_iv2])
        p.step(z2_eng, lambda e: e.tensor_tensor(out=r3(x2),
               in0=r3(x2), in1=bc16(self.iv2, nb), op=ALU.mult),
               reads=[bx2, self.b_iv2], writes=[bx2])
        for j in range(4):
            p.step("scalar", lambda e, j=j: e.activation(out=t[j][:, 0:width],
                   in_=x2, func=AF.Sign, bias=self.cst_t[:, j:j + 1]),
                   reads=[bx2, self.cst_b], writes=[bt[j]])
        p.step("vector", lambda e: e.tensor_scalar(out=t[4][:, 0:width],
               in0=x2, scalar1=MID2S[4], scalar2=2.0,
               op0=ALU.is_gt, op1=ALU.mult),
               reads=[bx2], writes=[bt[4]])
        p.step("vector", lambda e: e.tensor_scalar(out=t[5][:, 0:width],
               in0=x2, scalar1=MID2S[5], scalar2=2.0,
               op0=ALU.is_gt, op1=ALU.mult),
               reads=[bx2], writes=[bt[5]])
        p.step("vector", lambda e: e.tensor_tensor(out=t[4][:, 0:width],
               in0=t[4][:, 0:width], in1=t[5][:, 0:width], op=ALU.add),
               reads=[bt[4], bt[5]], writes=[bt[4]])
        p.step("vector", lambda e: e.tensor_scalar(out=t[5][:, 0:width],
               in0=x2, scalar1=MID2S[6], scalar2=4.0,
               op0=ALU.is_gt, op1=ALU.mult),
               reads=[bx2], writes=[bt[5]])
        p.step("vector", lambda e: e.tensor_tensor(out=t[4][:, 0:width],
               in0=t[4][:, 0:width], in1=t[5][:, 0:width], op=ALU.add),
               reads=[bt[4], bt[5]], writes=[bt[4]])
        p.step("vector", lambda e: e.tensor_tensor(out=t[0][:, 0:width],
               in0=t[0][:, 0:width], in1=t[1][:, 0:width], op=ALU.add),
               reads=[bt[0], bt[1]], writes=[bt[0]])
        p.step("vector", lambda e: e.tensor_tensor(out=t[2][:, 0:width],
               in0=t[2][:, 0:width], in1=t[3][:, 0:width], op=ALU.add),
               reads=[bt[2], bt[3]], writes=[bt[2]])
        p.step("vector", lambda e: e.tensor_tensor(out=t[0][:, 0:width],
               in0=t[0][:, 0:width], in1=t[2][:, 0:width], op=ALU.add),
               reads=[bt[0], bt[2]], writes=[bt[0]])
        p.step("vector", lambda e: e.tensor_scalar(out=t[0][:, 0:width],
               in0=t[0][:, 0:width], scalar1=0.5, scalar2=2.0,
               op0=ALU.mult, op1=ALU.add),
               reads=[bt[0]], writes=[bt[0]])
        p.step("vector", lambda e: e.tensor_tensor(out=t[0][:, 0:width],
               in0=t[0][:, 0:width], in1=t[4][:, 0:width], op=ALU.add),
               reads=[bt[0], bt[4]], writes=[bt[0]])
        p.step("vector", lambda e: e.tensor_tensor(out=t[0][:, 0:width],
               in0=t[0][:, 0:width], in1=sg[:, 0:width], op=ALU.mult),
               reads=[bt[0], bsg], writes=[bt[0]])
        p.step(dst_eng, lambda e: e.tensor_tensor(
               out=r3(dst_t[:, dst_off:dst_off + width]),
               in0=r3(t[0][:, 0:width]), in1=bc16(self.shh, nb), op=ALU.mult),
               reads=[bt[0], self.b_sh], writes=[dst_b])


class QuantPair:

    def __init__(self, nc, sb, qw, cst_t, cst_b):
        self.qs = [Quant(nc, sb, qw, cst_t, cst_b) for _ in range(2)]
        self.i = 0
        self.qw = qw

    def emit_multi(self, p, src_t, src_b, dst_t, dst_b, width, src_off=0,
                   dst_off=0, z2_eng="vector", dst_eng="gpsimd"):
        off = 0
        while off < width:
            w = min(self.qw, width - off)
            self.qs[self.i % 2].emit(p, src_t, src_b, dst_t, dst_b, w,
                                     src_off + off, dst_off + off,
                                     z2_eng, dst_eng)
            self.i += 1
            off += w


def emit_ln(p, src_t, src_b, dst_t, dst_b, ln):
    t = lambda n: ln[n + "_t"]
    b = lambda n: ln[n]
    p.step("vector", lambda e: e.bn_stats(t("st")[:, 0:6], src_t[:, 0:512]),
           reads=[src_b], writes=[b("st")])
    p.step("vector", lambda e: e.bn_stats(t("st")[:, 6:12], src_t[:, 512:1024]),
           reads=[src_b], writes=[b("st")])
    p.step("vector", lambda e: e.bn_aggr(t("mv")[:], t("st")[:]),
           reads=[b("st")], writes=[b("mv")])
    p.step("vector", lambda e: e.tensor_scalar(out=t("ve")[:], in0=t("mv")[:, 1:2],
           scalar1=1e-5, scalar2=None, op0=ALU.add),
           reads=[b("mv")], writes=[b("ve")])
    p.step("scalar", lambda e: e.activation(out=t("rs")[:], in_=t("ve")[:],
           func=AF.Sqrt),
           reads=[b("ve")], writes=[b("rs")])
    p.step("vector", lambda e: e.reciprocal(t("rs")[:], t("rs")[:]),
           reads=[b("rs")], writes=[b("rs")])
    p.step("vector", lambda e: e.tensor_scalar(out=dst_t[:], in0=src_t[:],
           scalar1=t("mv")[:, 0:1], scalar2=t("rs")[:, 0:1],
           op0=ALU.subtract, op1=ALU.mult),
           reads=[src_b, b("mv"), b("rs")], writes=[dst_b])


def build_program():
    nc = bass.Bass("TRN2", target_bir_lowering=False)
    x_h = nc.dram_tensor("x", [TOK, DM], F32, kind="ExternalInput").ap()
    wqkvT_h = nc.dram_tensor("wqkvT", [DM, 3 * DM], F32, kind="ExternalInput").ap()
    woT_h = nc.dram_tensor("woT", [DM, DM], F32, kind="ExternalInput").ap()
    wg_h = nc.dram_tensor("wg", [DI, DM], F32, kind="ExternalInput").ap()
    wu_h = nc.dram_tensor("wu", [DI, DM], F32, kind="ExternalInput").ap()
    wd_h = nc.dram_tensor("wd", [DM, DI], F32, kind="ExternalInput").ap()
    ident_h = nc.dram_tensor("ident", [P, P], BF16, kind="ExternalInput").ap()
    maskT_h = nc.dram_tensor("maskT", [P, P], BF16, kind="ExternalInput").ap()
    tot_h = nc.dram_tensor("tot", [P, NCH_SEQ], F32, kind="ExternalInput").ap()
    out_h = nc.dram_tensor("out", [TOK, DM], F32, kind="ExternalOutput").ap()
    x1_d = nc.dram_tensor("x1scr", [TOK, DM], F32, kind="Internal").ap()
    wgq_d = nc.dram_tensor("wgqscr", [DM, DI], BF16, kind="Internal").ap()
    wuq_d = nc.dram_tensor("wuqscr", [DM, DI], BF16, kind="Internal").ap()
    wdq_d = nc.dram_tensor("wdqscr", [DI, DM], BF16, kind="Internal").ap()

    with ExitStack() as st:
        sb = lambda n, s, d=F32: st.enter_context(nc.sbuf_tensor(n, s, d))
        mksem = lambda n: st.enter_context(nc.semaphore(n))

        p = Prog(nc)
        sem_handles = {}
        semnames = ["s_v", "s_a", "s_pe", "s_g"] + ["s_d%d" % i for i in range(NDMASEM)]
        for semn in semnames:
            h = mksem(semn)
            sem_handles[semn] = h
            p.sem(semn, h)
        nums = sorted(h.num for h in sem_handles.values())
        nc.gpsimd.sem_clear(range(nums[0], nums[-1] + 1))
        nc.all_engine_barrier()

        ident_t = sb("identt", [P, P], BF16); ident_b = Buf()
        maskT_t = sb("maskt", [P, P], BF16); maskT_b = Buf()
        invt_t = sb("invt", [P, NCH_SEQ], F32); invt_b = Buf()
        ln = {}
        for n, wdt in (("st", 12), ("mv", 2), ("ve", 1), ("rs", 1)):
            ln[n + "_t"] = sb("ln_" + n, [P, wdt], F32)
            ln[n] = Buf()
        x1d_bufs = [Buf() for _ in range(NCH)]
        cst_t = sb("cstt", [P, 8], F32); cst_b = Buf()
        for i, v in enumerate([-MID2S[0], -MID2S[1], -MID2S[2], -MID2S[3],
                               1.0 / 144.0]):
            p.step("gpsimd", lambda e, i=i, v=v: e.memset(cst_t[:, i:i + 1], v),
                   writes=[cst_b])
        wgq_db, wuq_db, wdq_db = Buf(), Buf(), Buf()

        with ExitStack() as stA:
            sbA = lambda n, s, d=F32: stA.enter_context(nc.sbuf_tensor(n, s, d))
            psA = lambda n, s, d=F32: stA.enter_context(nc.psum_tensor(n, s, d))

            q = QuantPair(nc, sbA, 704, cst_t, cst_b)

            state_t = sbA("state", [P, NHP * 64]); state_b = Buf()
            stateb_t = sbA("stateb", [P, NHP * 64], BF16); stateb_b = Buf()

            p.step("sync", lambda e: e.dma_start(out=ident_t[:], in_=ident_h[:, :]),
                   writes=[ident_b], dma=True)
            p.step("sync", lambda e: e.dma_start(out=maskT_t[:], in_=maskT_h[:, :]),
                   writes=[maskT_b], dma=True)
            p.step("sync", lambda e: e.dma_start(out=invt_t[:], in_=tot_h[:, :]),
                   writes=[invt_b], dma=True)
            p.step("vector", lambda e: e.reciprocal(invt_t[:], invt_t[:]),
                   reads=[invt_b], writes=[invt_b])
            p.step("vector", lambda e: e.tensor_scalar(out=invt_t[:], in0=invt_t[:],
                   scalar1=0.125, scalar2=None, op0=ALU.mult),
                   reads=[invt_b], writes=[invt_b])

            wqkv_t = sbA("wqkvb", [P, NKC, 3 * DM], BF16); wqkv_b = Buf()
            wo_t = sbA("wob", [P, NKC, DM], BF16); wo_b = Buf()
            wstg_t = sbA("wstg", [P, 3 * DM]); wstg_b = Buf()
            for kc in range(NKC):
                p.step("sync", lambda e, kc=kc: e.dma_start(
                    out=wstg_t[:, 0:3 * DM], in_=wqkvT_h[kc * P:(kc + 1) * P, :]),
                    writes=[wstg_b], dma=True)
                p.step("vector", lambda e, kc=kc: e.tensor_copy(wqkv_t[:, kc, :], wstg_t[:, 0:3 * DM]),
                       reads=[wstg_b], writes=[wqkv_b])
            for kc in range(NKC):
                p.step("sync", lambda e, kc=kc: e.dma_start(
                    out=wstg_t[:, 0:DM], in_=woT_h[kc * P:(kc + 1) * P, :]),
                    writes=[wstg_b], dma=True)
                p.step("vector", lambda e, kc=kc: e.tensor_copy(wo_t[:, kc, :], wstg_t[:, 0:DM]),
                       reads=[wstg_b], writes=[wo_b])

            x_t = [sbA("xin%d" % i, [P, DM]) for i in range(3)]
            x_b = [Buf() for _ in range(3)]
            lnb_t = [sbA("lnb%d" % i, [P, DM], BF16) for i in range(2)]
            lnb_b = [Buf(), Buf()]
            lnfm_t = [sbA("lnfm%d" % i, [P, NKC, P], BF16) for i in range(2)]
            lnfm_b = [Buf(), Buf()]
            q_tm_t = [sbA("qtm%d" % i, [P, DM], BF16) for i in range(2)]
            q_tm_b = [Buf(), Buf()]
            k_tm_t = [sbA("ktm%d" % i, [P, DM], BF16) for i in range(2)]
            k_tm_b = [Buf(), Buf()]
            v_tm_t = [sbA("vtm%d" % i, [P, DM], BF16) for i in range(2)]
            v_tm_b = [Buf(), Buf()]
            q_fm_t = [sbA("qfm%d" % i, [P, NKC, P], BF16) for i in range(2)]
            q_fm_b = [Buf(), Buf()]
            k_fm_t = [sbA("kfm%d" % i, [P, NKC, P], BF16) for i in range(2)]
            k_fm_b = [Buf(), Buf()]
            scT_t = sbA("scT", [P, 4, P], BF16); scT_b = [Buf(), Buf()]
            ao_t = sbA("aofm", [P, NKC, P], BF16); ao_b = Buf()
            x1_t = [sbA("x1t%d" % i, [P, DM]) for i in range(2)]
            x1_b = [Buf(), Buf()]
            wpin_t = sbA("wpin", [P, DI]); wpin_b = Buf()
            wq_t = [sbA("wqtile%d" % i, [P, DI], BF16) for i in range(2)]
            wq_b = [Buf(), Buf()]
            wtr_t = [sbA("wtrtile%d" % i, [P, 4, P], BF16) for i in range(2)]
            wtr_b = [Buf(), Buf()]

            pp = [psA("ps_pp%d" % i, [P, 512]) for i in range(2)]
            pp_b = [Buf(), Buf()]
            ps_t = psA("ps_tr", [P, 4, P], BF16); ps_t_b = Buf()
            ps_tB = psA("ps_trB", [P, 4, P], BF16); ps_tB_b = Buf()
            pst = [ps_t, ps_tB]; pstb = [ps_t_b, ps_tB_b]
            ps_scA = psA("ps_scA", [P, P]); ps_scA_b = Buf()
            ps_scB = psA("ps_scB", [P, P]); ps_scB_b = Buf()
            ps_o = psA("ps_o", [P, 4, P]); ps_o_b = [Buf() for _ in range(4)]
            ps_kv = psA("ps_kv", [P, 512]); ps_kv_b = Buf()

            prep_units = []
            for w_h, wq_dst, dstb, nrt, cols in (
                    (wg_h, wgq_d, wgq_db, NKI, DM),
                    (wu_h, wuq_d, wuq_db, NKI, DM)):
                for rt in range(nrt):
                    prep_units.append((w_h, wq_dst, dstb, rt, cols, None))
            for rt in range(NKC):
                for half in range(2):
                    prep_units.append((wd_h, wdq_d, wdq_db, rt, DI, half))

            def emit_prep_q(unit, pb):
                w_h, wq_dst, dstb, rt, cols, half = unit
                if half is None or half == 0:
                    p.step("sync", lambda e: e.dma_start(
                        out=wpin_t[:, 0:cols], in_=w_h[rt * P:(rt + 1) * P, :]),
                        writes=[wpin_b], dma=True)
                if half is None:
                    q.emit_multi(p, wpin_t, wpin_b, wq_t[pb], wq_b[pb], cols,
                                 z2_eng="gpsimd", dst_eng="gpsimd")
                else:
                    q.emit_multi(p, wpin_t, wpin_b, wq_t[pb], wq_b[pb], HI,
                                 src_off=half * HI, dst_off=half * HI,
                                 z2_eng="gpsimd", dst_eng="gpsimd")

            def emit_prep_t_byidx(j):
                emit_prep_t(prep_units[j], j % 2)

            def emit_prep_t(unit, pb):
                w_h, wq_dst, dstb, rt, cols, half = unit
                if half is None:
                    ccs = list(range(cols // P))
                else:
                    ccs = list(range(half * (HI // P), (half + 1) * (HI // P)))
                for g in range((len(ccs) + 3) // 4):
                    grp = ccs[g * 4:(g + 1) * 4]
                    pt, ptb = pst[g % 2], pstb[g % 2]
                    wt, wb = wtr_t[g % 2], wtr_b[g % 2]
                    for i, cc in enumerate(grp):
                        p.step("tensor", lambda e, cc=cc, i=i, pb=pb, pt=pt: e.transpose(
                            pt[:, i, :], wq_t[pb][:, cc * P:(cc + 1) * P], ident_t[:]),
                            reads=[wq_b[pb], ident_b], writes=[ptb])
                    p.step("vector", lambda e, n=len(grp), pt=pt, wt=wt: e.tensor_copy(
                        wt[:, 0:n, :], pt[:, 0:n, :]),
                        reads=[ptb], writes=[wb])
                    for i, cc in enumerate(grp):
                        p.step("sync", lambda e, cc=cc, i=i, wt=wt: e.dma_start(
                            out=wq_dst[cc * P:(cc + 1) * P, rt * P:(rt + 1) * P],
                            in_=wt[:, i, :]),
                            reads=[wb], writes=[dstb], dma=True)

            def a_pre(ct):
                xb = x_b[ct % 3]
                xt = x_t[ct % 3]
                p.step("sync", lambda e, ct=ct, xt=xt: e.dma_start(
                    out=xt[:], in_=x_h[ct * P:(ct + 1) * P, :]),
                    writes=[xb], dma=True)

            def a_prep(ct):
                if not A_PREP:
                    return
                xb, xt = x_b[ct % 3], x_t[ct % 3]
                lb, lt = lnb_b[ct % 2], lnb_t[ct % 2]
                fb, ft = lnfm_b[ct % 2], lnfm_t[ct % 2]
                emit_ln(p, xt, xb, lt, lb, ln)

            def a1(ct):
                cs = ct % NCH_SEQ
                fb, ft = lnfm_b[ct % 2], lnfm_t[ct % 2]
                d = ct % 2
                dp = (ct - 1) % 2
                if ct > 0 and A_TR:
                    bi = 0
                    for src_tt, src_bb, dst_tt, dst_bb in ((q_tm_t[dp], q_tm_b[dp], q_fm_t[dp], q_fm_b[dp]),
                                                           (k_tm_t[dp], k_tm_b[dp], k_fm_t[dp], k_fm_b[dp])):
                        for g in range(2):
                            pt, pb = pst[bi % 2], pstb[bi % 2]
                            bi += 1
                            for i in range(4):
                                cc = g * 4 + i
                                p.step("tensor", lambda e, cc=cc, i=i, src_tt=src_tt, pt=pt: e.transpose(
                                    pt[:, i, :], src_tt[:, cc * P:(cc + 1) * P], ident_t[:]),
                                    reads=[src_bb, ident_b], writes=[pb])
                            p.step("vector", lambda e, g=g, dst_tt=dst_tt, pt=pt: e.tensor_copy(
                                dst_tt[:, g * 4:(g + 1) * 4, :], pt[:]),
                                reads=[pb], writes=[dst_bb])
                if A_PREP:
                    lb2, lt2 = lnb_b[ct % 2], lnb_t[ct % 2]
                    for g in range(2):
                        pt, pb = pst[g % 2], pstb[g % 2]
                        for i in range(4):
                            cc = g * 4 + i
                            p.step("tensor", lambda e, cc=cc, i=i, lt2=lt2, pt=pt: e.transpose(
                                pt[:, i, :], lt2[:, cc * P:(cc + 1) * P], ident_t[:]),
                                reads=[lb2, ident_b], writes=[pb])
                        p.step("vector", lambda e, g=g, ft=ft, pt=pt: e.tensor_copy(
                            ft[:, g * 4:(g + 1) * 4, :], pt[:]),
                            reads=[pb], writes=[fb])
                for slab in range(6 if A_QKV else 0):
                    pg = slab % 2
                    def mm_slab(e, slab=slab, ft=ft, pg=pg):
                        for kc in range(NKC):
                            i = e.matmul(pp[pg][:], ft[:, kc, :],
                                         wqkv_t[:, kc, slab * 512:(slab + 1) * 512],
                                         start=(kc == 0), stop=(kc == NKC - 1))
                        return i
                    p.step("tensor", mm_slab, reads=[fb, wqkv_b], writes=[pp_b[pg]])
                    if slab < 2:
                        p.step("scalar", lambda e, slab=slab, cs=cs, d=d, pg=pg: e.activation(
                            out=q_tm_t[d][:, slab * 512:(slab + 1) * 512],
                            in_=pp[pg][:], func=AF.Copy,
                            scale=invt_t[:, cs:cs + 1]),
                            reads=[pp_b[pg], invt_b], writes=[q_tm_b[d]])
                    elif slab < 4:
                        p.step("scalar", lambda e, slab=slab, d=d, pg=pg: e.activation(
                            out=k_tm_t[d][:, (slab - 2) * 512:(slab - 1) * 512],
                            in_=pp[pg][:], func=AF.Sign),
                            reads=[pp_b[pg]], writes=[k_tm_b[d]])
                    else:
                        p.step("scalar", lambda e, slab=slab, d=d, pg=pg: e.activation(
                            out=v_tm_t[d][:, (slab - 4) * 512:(slab - 3) * 512],
                            in_=pp[pg][:], func=AF.Sign),
                            reads=[pp_b[pg]], writes=[v_tm_b[d]])

            def a2(ct):
                cs = ct % NCH_SEQ
                xb, xt = x_b[ct % 3], x_t[ct % 3]
                d = ct % 2
                qfm, qfmb = q_fm_t[d], q_fm_b[d]
                kfm, kfmb = k_fm_t[d], k_fm_b[d]
                ktm, ktmb = k_tm_t[d], k_tm_b[d]
                vtm, vtmb = v_tm_t[d], v_tm_b[d]
                if cs == 0 and A_ATT:
                    p.step("gpsimd", lambda e: e.memset(state_t[:], 0.0),
                           writes=[state_b])
                    p.step("gpsimd", lambda e: e.memset(stateb_t[:], 0.0),
                           writes=[stateb_b])
                scp = [(ps_scA, ps_scB, [ps_scA_b, ps_scB_b]),
                       (pp[0][:, 0:P], pp[1][:, 0:P], [pp_b[0], pp_b[1]])]

                def emit_sc(hp):
                    par = hp % 2
                    pa, pb, bufs = scp[par]
                    def mm_sc(e, hp=hp, pa=pa, pb=pb):
                        e.matmul(pa[:], kfm[0:64, hp, :], qfm[0:64, hp, :],
                                 start=True, stop=True)
                        return e.matmul(pb[:], kfm[64:128, hp, :],
                                        qfm[64:128, hp, :], start=True, stop=True)
                    p.step("tensor", mm_sc, reads=[kfmb, qfmb], writes=bufs)

                def emit_mask(hp):
                    par = hp % 2
                    pa, pb, bufs = scp[par]
                    p.step("vector", lambda e, par=par, pa=pa: e.tensor_tensor(
                        out=scT_t[:, 2 * par, :], in0=pa[:], in1=maskT_t[:],
                        op=ALU.mult),
                        reads=[bufs[0], maskT_b], writes=[scT_b[par]])
                    p.step("vector", lambda e, par=par, pb=pb: e.tensor_tensor(
                        out=scT_t[:, 2 * par + 1, :], in0=pb[:], in1=maskT_t[:],
                        op=ALU.mult),
                        reads=[bufs[1], maskT_b], writes=[scT_b[par]])

                def emit_att(hp):
                    par = hp % 2
                    def mm_att(e, hp=hp, par=par):
                        e.matmul(ps_o[0:64, hp % 4, :], vtm[:, hp * P:hp * P + 64],
                                 scT_t[:, 2 * par, :], start=True, stop=False)
                        e.matmul(ps_o[0:64, hp % 4, :], stateb_t[0:64, hp * 64:(hp + 1) * 64],
                                 qfm[0:64, hp, :], start=False, stop=True)
                        e.matmul(ps_o[64:128, hp % 4, :], vtm[:, hp * P + 64:hp * P + 128],
                                 scT_t[:, 2 * par + 1, :], start=True, stop=False)
                        return e.matmul(ps_o[64:128, hp % 4, :], stateb_t[64:128, hp * 64:(hp + 1) * 64],
                                        qfm[64:128, hp, :], start=False, stop=True)
                    p.step("tensor", mm_att, reads=[vtmb, scT_b[par], stateb_b, qfmb],
                           writes=[ps_o_b[hp % 4]])

                def emit_ckv(hp):
                    def mm_ckv(e, hp=hp):
                        e.matmul(ps_kv[0:64, hp * 64:(hp + 1) * 64],
                                 ktm[:, hp * P:hp * P + 64],
                                 vtm[:, hp * P:hp * P + 64], start=True, stop=True)
                        return e.matmul(ps_kv[64:128, hp * 64:(hp + 1) * 64],
                                        ktm[:, hp * P + 64:hp * P + 128],
                                        vtm[:, hp * P + 64:hp * P + 128],
                                        start=True, stop=True)
                    p.step("tensor", mm_ckv, reads=[ktmb, vtmb], writes=[ps_kv_b])

                if A_ATT:
                    emit_sc(0)
                    for hp in range(NHP):
                        emit_mask(hp)
                        if hp + 1 < NHP:
                            emit_sc(hp + 1)
                        emit_att(hp)
                        emit_ckv(hp)
                        if hp % 4 == 3:
                            g = hp // 4
                            p.step("scalar", lambda e, g=g: e.copy(
                                out=ao_t[:, g * 4:(g + 1) * 4, :], in_=ps_o[:, 0:4, :]),
                                reads=[ps_o_b[0], ps_o_b[1], ps_o_b[2], ps_o_b[3]],
                                writes=[ao_b])
                    p.step("vector", lambda e: e.tensor_tensor(
                        out=state_t[:], in0=state_t[:], in1=ps_kv[:], op=ALU.add),
                        reads=[ps_kv_b, state_b], writes=[state_b])
                    p.step("vector", lambda e: e.tensor_copy(stateb_t[:], state_t[:]),
                           reads=[state_b], writes=[stateb_b])
                x1b, x1t = x1_b[ct % 2], x1_t[ct % 2]
                for slab in range(2 if A_WO else 0):
                    pg = slab % 2
                    def mm_wo(e, slab=slab, pg=pg):
                        for kc in range(NKC):
                            i = e.matmul(pp[pg][:], ao_t[:, kc, :],
                                         wo_t[:, kc, slab * 512:(slab + 1) * 512],
                                         start=(kc == 0), stop=(kc == NKC - 1))
                        return i
                    p.step("tensor", mm_wo, reads=[ao_b, wo_b], writes=[pp_b[pg]])
                    p.step("vector", lambda e, slab=slab, xt=xt, x1t=x1t, pg=pg: e.tensor_tensor(
                        out=x1t[:, slab * 512:(slab + 1) * 512], in0=pp[pg][:],
                        in1=xt[:, slab * 512:(slab + 1) * 512], op=ALU.add),
                        reads=[pp_b[pg], xb], writes=[x1b])
                p.step("sync", lambda e, ct=ct, x1t=x1t: e.dma_start(
                    out=x1_d[ct * P:(ct + 1) * P, :], in_=x1t[:]),
                    reads=[x1b], writes=[x1d_bufs[ct]], dma=True)

            if A_ON:
                a_pre(0)
                a_pre(1)
                a_prep(0)
                a_pre(2)
                a_prep(1)
                a1(0)
            pu = 0
            pending_t = []
            for ct in range(NCH):
                if PREP_ON:
                    while pending_t:
                        emit_prep_t_byidx(pending_t.pop(0))
                if A_ON:
                    if ct + 2 < NCH:
                        a_prep(ct + 2)
                    if ct + 1 < NCH:
                        a1(ct + 1)
                    a2(ct)
                    if ct + 3 < NCH:
                        a_pre(ct + 3)
                if PREP_ON:
                    while pu < len(prep_units) and pu < (ct + 1) * len(prep_units) // (NCH - 4):
                        emit_prep_q(prep_units[pu], pu % 2)
                        pending_t.append(pu)
                        pu += 1
            if PREP_ON:
                while pu < len(prep_units):
                    while pending_t:
                        emit_prep_t_byidx(pending_t.pop(0))
                    emit_prep_q(prep_units[pu], pu % 2)
                    pending_t.append(pu)
                    pu += 1
                while pending_t:
                    emit_prep_t_byidx(pending_t.pop(0))

            with nc.Block() as blockA:
                p.flush(blockA)

        with ExitStack() as stB:
            sbB = lambda n, s, d=F32: stB.enter_context(nc.sbuf_tensor(n, s, d))
            psB = lambda n, s, d=F32: stB.enter_context(nc.psum_tensor(n, s, d))

            qB = QuantPair(nc, sbB, 576, cst_t, cst_b)

            wgq_t = sbB("wgqb", [P, NKC, DI], BF16)
            wuq_t = sbB("wuqb", [P, NKC, DI], BF16)
            wdq_t = sbB("wdqb", [P, NKI, DM], BF16)
            wgq_bs = [Buf() for _ in range(NKC)]
            wuq_bs = [Buf() for _ in range(NKC)]
            wdq_bs = [Buf() for _ in range(NKI)]
            for kc in range(NKC if B_ON else 0):
                p.step("sync", lambda e, kc=kc: e.dma_start(
                    out=wgq_t[:, kc, :], in_=wgq_d[kc * P:(kc + 1) * P, :]),
                    reads=[wgq_db], writes=[wgq_bs[kc]], dma=True)
                p.step("sync", lambda e, kc=kc: e.dma_start(
                    out=wuq_t[:, kc, :], in_=wuq_d[kc * P:(kc + 1) * P, :]),
                    reads=[wuq_db], writes=[wuq_bs[kc]], dma=True)
            for kc in range(NKI if B_ON else 0):
                p.step("sync", lambda e, kc=kc: e.dma_start(
                    out=wdq_t[:, kc, :], in_=wdq_d[kc * P:(kc + 1) * P, :]),
                    reads=[wdq_db], writes=[wdq_bs[kc]], dma=True)

            x1b_t = [sbB("x1in%d" % i, [P, DM]) for i in range(3)]
            x1b_b = [Buf() for _ in range(3)]
            ln2_t = sbB("ln2", [P, DM]); ln2_b = Buf()
            xq_t = [sbB("xq%d" % i, [P, DM], BF16) for i in range(2)]
            xq_b = [Buf(), Buf()]
            xq_fm_t = [sbB("xqfm%d" % i, [P, NKC, P], BF16) for i in range(2)]
            xq_fm_b = [Buf(), Buf()]
            h_t = [sbB("hhalf%d" % i, [P, HI]) for i in range(2)]
            h_b = [Buf(), Buf()]
            hq_t = [[sbB("hq%d_%d" % (j, i), [P, HI], BF16) for i in range(2)]
                    for j in range(2)]
            hq_b = [[Buf(), Buf()], [Buf(), Buf()]]
            hq_fm_t = [sbB("hqfm%d" % i, [P, NKI, P], BF16) for i in range(2)]
            hq_fm_b = [Buf(), Buf()]
            ps_g = [psB("ps_g%d" % i, [P, 512]) for i in range(2)]
            ps_g_b = [Buf(), Buf()]
            ps_u = [psB("ps_u%d" % i, [P, 512]) for i in range(2)]
            ps_u_b = [Buf(), Buf()]
            ps_d = [psB("ps_d%d" % i, [P, 512]) for i in range(2)]
            ps_d_b = [Buf(), Buf()]
            ps_t2 = psB("ps_tr2", [P, 4, P], BF16); ps_t2_b = Buf()
            ps_t2B = psB("ps_tr2B", [P, 4, P], BF16); ps_t2B_b = Buf()
            pst2 = [ps_t2, ps_t2B]; pst2b = [ps_t2_b, ps_t2B_b]

            GSLABS = [(0, 512), (512, 512), (1024, 384)]

            out_toks = []

            def b_pre(ct):
                xb, xt = x1b_b[ct % 3], x1b_t[ct % 3]
                p.step("sync", lambda e, ct=ct, xt=xt: e.dma_start(
                    out=xt[:], in_=x1_d[ct * P:(ct + 1) * P, :]),
                    reads=[x1d_bufs[ct]], writes=[xb], dma=True)

            def b_prep(ct):
                xb, xt = x1b_b[ct % 3], x1b_t[ct % 3]
                emit_ln(p, xt, xb, ln2_t, ln2_b, ln)
                xqb, xqt = xq_b[ct % 2], xq_t[ct % 2]
                qB.emit_multi(p, ln2_t, ln2_b, xqt, xqb, DM,
                              z2_eng="vector", dst_eng="vector")

            def b1(ct):
                fb, ft = xq_fm_b[ct % 2], xq_fm_t[ct % 2]
                d = ct % 2
                hfmt, hfmb = hq_fm_t[d], hq_fm_b[d]
                xqb, xqt = xq_b[ct % 2], xq_t[ct % 2]
                for g in range(2):
                    pt, pb = pst2[g % 2], pst2b[g % 2]
                    for i in range(4):
                        cc = g * 4 + i
                        p.step("tensor", lambda e, cc=cc, i=i, xqt=xqt, pt=pt: e.transpose(
                            pt[:, i, :], xqt[:, cc * P:(cc + 1) * P], ident_t[:]),
                            reads=[xqb, ident_b], writes=[pb])
                    p.step("vector", lambda e, g=g, ft=ft, pt=pt: e.tensor_copy(
                        ft[:, g * 4:(g + 1) * 4, :], pt[:]),
                        reads=[pb], writes=[fb])
                for half in range(2):
                    hb, ht = h_b[half], h_t[half]
                    hqb, hqt = hq_b[d][half], hq_t[d][half]
                    for si, (off, wl) in enumerate(GSLABS):
                        pg = si % 2

                        def mm_g(e, half=half, ft=ft, off=off, wl=wl, pg=pg):
                            for kc in range(NKC):
                                i = e.matmul(ps_g[pg][:, 0:wl], ft[:, kc, :],
                                             wgq_t[:, kc, half * HI + off:half * HI + off + wl],
                                             start=(kc == 0), stop=(kc == NKC - 1))
                            return i
                        p.step("tensor", mm_g, reads=[fb] + wgq_bs, writes=[ps_g_b[pg]])

                        def mm_u(e, half=half, ft=ft, off=off, wl=wl, pg=pg):
                            for kc in range(NKC):
                                i = e.matmul(ps_u[pg][:, 0:wl], ft[:, kc, :],
                                             wuq_t[:, kc, half * HI + off:half * HI + off + wl],
                                             start=(kc == 0), stop=(kc == NKC - 1))
                            return i
                        p.step("tensor", mm_u, reads=[fb] + wuq_bs, writes=[ps_u_b[pg]])
                        p.step("scalar", lambda e, ht=ht, off=off, wl=wl, pg=pg: e.activation(
                            out=ht[:, off:off + wl], in_=ps_g[pg][:, 0:wl], func=AF.Silu),
                            reads=[ps_g_b[pg]], writes=[hb])
                        p.step("vector", lambda e, ht=ht, off=off, wl=wl, pg=pg: e.tensor_tensor(
                            out=ht[:, off:off + wl], in0=ps_u[pg][:, 0:wl],
                            in1=ht[:, off:off + wl], op=ALU.mult),
                            reads=[ps_u_b[pg], hb], writes=[hb])
                    qB.emit_multi(p, ht, hb, hqt, hqb, HI,
                                  z2_eng="gpsimd", dst_eng="gpsimd")

            def b2(ct):
                xb, xt = x1b_b[ct % 3], x1b_t[ct % 3]
                d = ct % 2
                hfmt, hfmb = hq_fm_t[d], hq_fm_b[d]
                bi = 0
                for half in range(2):
                    hqb, hqt = hq_b[d][half], hq_t[d][half]
                    ccs = list(range(11))
                    for g in range(3):
                        grp = ccs[g * 4:(g + 1) * 4]
                        if not grp:
                            break
                        pt, pb = pst2[bi % 2], pst2b[bi % 2]
                        bi += 1
                        for i, cc in enumerate(grp):
                            p.step("tensor", lambda e, cc=cc, i=i, hqt=hqt, pt=pt: e.transpose(
                                pt[:, i, :], hqt[:, cc * P:(cc + 1) * P], ident_t[:]),
                                reads=[hqb, ident_b], writes=[pb])
                        p.step("vector", lambda e, g=g, grp=grp, half=half, hfmt=hfmt, pt=pt: e.tensor_copy(
                            hfmt[:, half * 11 + grp[0]:half * 11 + grp[0] + len(grp), :],
                            pt[:, 0:len(grp), :]),
                            reads=[pb], writes=[hfmb])
                for slab in range(2):
                    pg = slab % 2
                    def mm_d(e, slab=slab, hfmt=hfmt, pg=pg):
                        for kc in range(NKI):
                            i = e.matmul(ps_d[pg][:], hfmt[:, kc, :],
                                         wdq_t[:, kc, slab * 512:(slab + 1) * 512],
                                         start=(kc == 0), stop=(kc == NKI - 1))
                        return i
                    p.step("tensor", mm_d, reads=[hfmb] + wdq_bs, writes=[ps_d_b[pg]])
                    p.step("vector", lambda e, slab=slab, xt=xt, pg=pg: e.tensor_tensor(
                        out=xt[:, slab * 512:(slab + 1) * 512], in0=ps_d[pg][:],
                        in1=xt[:, slab * 512:(slab + 1) * 512], op=ALU.add),
                        reads=[ps_d_b[pg], xb], writes=[xb])
                out_toks.append(p.step("sync", lambda e, ct=ct, xt=xt: e.dma_start(
                    out=out_h[ct * P:(ct + 1) * P, :], in_=xt[:]),
                    reads=[xb], dma=True))

            if B_ON:
                b_pre(0)
                b_pre(1)
                b_prep(0)
                b_pre(2)
                b_prep(1)
                b1(0)
                for ct in range(NCH):
                    if ct + 2 < NCH:
                        b_prep(ct + 2)
                    if ct + 1 < NCH:
                        b1(ct + 1)
                    b2(ct)
                    if ct + 3 < NCH:
                        b_pre(ct + 3)

            p.step("gpsimd", lambda e: e.memset(ln["st_t"][:, 0:1], 0.0),
                   extra=out_toks, writes=[ln["st"]])

            with nc.Block() as blockB:
                p.flush(blockB)
    return nc


_NC_CACHE = {}


def kernel(L0_attn_x, L0_attn_W_qkv, L0_attn_W_o, L0_ffn_w_gate, L0_ffn_w_up,
           L0_ffn_w_down):
    x = np.ascontiguousarray(np.asarray(L0_attn_x, dtype=np.float32))
    wqkvT = np.ascontiguousarray(np.asarray(L0_attn_W_qkv, dtype=np.float32).T)
    woT = np.ascontiguousarray(np.asarray(L0_attn_W_o, dtype=np.float32).T)
    wg = np.ascontiguousarray(np.asarray(L0_ffn_w_gate, dtype=np.float32))
    wu = np.ascontiguousarray(np.asarray(L0_ffn_w_up, dtype=np.float32))
    wd = np.ascontiguousarray(np.asarray(L0_ffn_w_down, dtype=np.float32))
    ident = np.eye(P, dtype=np.float32).astype(ml_dtypes.bfloat16)
    maskT = np.triu(np.ones((P, P), dtype=np.float32)).astype(ml_dtypes.bfloat16)
    tot = (np.arange(1, T + 1, dtype=np.float32).reshape(NCH_SEQ, P).T).copy()

    if "nc" not in _NC_CACHE:
        _NC_CACHE["nc"] = build_program()
    nc = _NC_CACHE["nc"]

    in_maps = []
    for c in range(NCORES):
        xs = np.ascontiguousarray(
            x[c * SEQ_PER_CORE:(c + 1) * SEQ_PER_CORE].reshape(TOK, DM))
        in_maps.append({"x": xs, "wqkvT": wqkvT, "woT": woT, "wg": wg,
                        "wu": wu, "wd": wd, "ident": ident, "maskT": maskT,
                        "tot": tot})
    res = run_bass_kernel_spmd(nc, in_maps, core_ids=list(range(NCORES)))
    out = np.concatenate([np.asarray(r["out"]) for r in res.results], axis=0)
    return out.reshape(B_FULL, T, DM).astype(np.float32)


# version 12
# speedup vs baseline: 1.0625x; 1.0625x over previous
import numpy as np
import ml_dtypes
import concourse.bass as bass
import concourse.mybir as mybir
from concourse.bass_utils import run_bass_kernel_spmd
from contextlib import ExitStack

F32 = mybir.dt.float32
BF16 = mybir.dt.bfloat16
AF = mybir.ActivationFunctionType
ALU = mybir.AluOpType
AX = mybir.AxisListType

P = 128
B_FULL, T, DM = 16, 4096, 1024
NCORES = 8
SEQ_PER_CORE = B_FULL // NCORES
TOK = SEQ_PER_CORE * T
NCH_SEQ = T // P
NCH = TOK // P
DI = 2816
HI = DI // 2
NKC = DM // P
NKI = DI // P
NHP = 8

MID2S = [(m * m) / 36.0 for m in (0.25, 0.75, 1.25, 1.75, 2.5, 3.5, 5.0)]

ENGS = ("sync", "vector", "scalar", "tensor", "gpsimd")
ENG_SEM = {"sync": "s_dma", "vector": "s_v", "scalar": "s_a",
           "tensor": "s_pe", "gpsimd": "s_g"}
NDMASEM = 8
PREP_ON = True
A_ON = True
B_ON = True
A_PREP = True
A_QKV = True
A_TR = True
A_ATT = True
A_WO = True
A_ATT_LVL = 6


class Buf:
    __slots__ = ("w", "r")

    def __init__(self):
        self.w = None
        self.r = []


class Prog:
    def __init__(self, nc):
        self.nc = nc
        self.steps = {e: [] for e in ENGS}
        self.sems = {}
        self.cnt = {}
        self.dma_idx = 0

    def sem(self, name, handle):
        self.sems[name] = handle
        self.cnt[name] = 0

    def step(self, eng, emit, reads=(), writes=(), extra=(), dma=False):
        deps = []
        for b in reads:
            if b.w is not None:
                deps.append(b.w)
        for b in writes:
            if b.w is not None:
                deps.append(b.w)
            deps.extend(b.r)
        deps.extend(extra)
        if dma:
            idx = self.dma_idx
            self.dma_idx += 1
            semn = "s_d%d" % (idx % NDMASEM)
            self.cnt[semn] += 16
            if self.cnt[semn] > 16:
                deps.append((eng, semn, self.cnt[semn] - 16, True))
            tok = (eng, semn, self.cnt[semn], False)
            self.steps[eng].append((tuple(deps), emit, semn, 16))
        else:
            semn = ENG_SEM[eng]
            self.cnt[semn] += 1
            tok = (eng, semn, self.cnt[semn], False)
            self.steps[eng].append((tuple(deps), emit, semn, 1))
        for b in reads:
            b.r.append(tok)
        for b in writes:
            b.w = tok
            b.r = []
        return tok

    def flush(self, block):
        if not hasattr(self, "waited"):
            self.waited = {e: dict.fromkeys(self.sems, 0) for e in ENGS}

        def mk(eng_name, steps):
            def body(eng):
                w = self.waited[eng_name]
                for deps, emit, semn, inc in steps:
                    for (peng, psem, pcnt, psync) in deps:
                        if w[psem] >= pcnt:
                            continue
                        eng.wait_ge(self.sems[psem], pcnt)
                        w[psem] = pcnt
                    inst = emit(eng)
                    inst.then_inc(self.sems[semn], inc)
            return body

        for e in ENGS:
            if self.steps[e]:
                getattr(block, e)(mk(e, self.steps[e]))
        self.steps = {e: [] for e in ENGS}


def r3(t, k=16):
    return t.rearrange("p (b k) -> p b k", k=k)


def bc16(t, nb):
    return t[:, 0:nb].unsqueeze(2).broadcast_to([P, nb, 16])


class Quant:

    _uid = [0]

    def __init__(self, nc, sb, qw, cst_t, cst_b):
        self.qw = qw
        self.cst_t = cst_t
        self.cst_b = cst_b
        nbw = qw // 16
        u = "%d" % Quant._uid[0]
        Quant._uid[0] += 1
        self.sg = sb("q_sg" + u, [P, qw], BF16)
        self.t = [sb("q_t%d_" % i + u, [P, qw], BF16) for i in range(6)]
        self.bm2 = sb("q_bm2" + u, [P, nbw], F32)
        self.iv2 = sb("q_iv2" + u, [P, nbw], F32)
        self.shh = sb("q_sh" + u, [P, nbw], BF16)
        self.b_sg = Buf()
        self.b_t = [Buf() for _ in range(6)]
        self.b_bm2 = Buf()
        self.b_iv2 = Buf()
        self.b_sh = Buf()

    def emit_multi(self, p, src_t, src_b, dst_t, dst_b, width, src_off=0,
                   dst_off=0, z2_eng="vector", dst_eng="gpsimd"):
        off = 0
        while off < width:
            w = min(self.qw, width - off)
            self.emit(p, src_t, src_b, dst_t, dst_b, w, src_off + off,
                      dst_off + off, z2_eng, dst_eng)
            off += w


    def emit(self, p, src_t, src_b, dst_t, dst_b, width, src_off=0, dst_off=0,
             z2_eng="vector", dst_eng="gpsimd"):
        nb = width // 16
        sg, t = self.sg, self.t
        bsg, bt = self.b_sg, self.b_t
        x2 = src_t[:, src_off:src_off + width]
        bx2 = src_b
        p.step("vector", lambda e: e.tensor_reduce(out=self.bm2[:, 0:nb],
               in_=r3(x2), axis=AX.X, op=ALU.max, apply_absolute_value=True),
               reads=[src_b], writes=[self.b_bm2])
        p.step("scalar", lambda e: e.activation(out=sg[:, 0:width], in_=x2,
                                                func=AF.Sign),
               reads=[src_b], writes=[bsg])
        p.step("scalar", lambda e: e.activation(out=x2, in_=x2, func=AF.Square),
               reads=[src_b], writes=[bx2])
        p.step("vector", lambda e: e.tensor_scalar(out=self.bm2[:, 0:nb],
               in0=self.bm2[:, 0:nb], scalar1=6e-12, scalar2=None, op0=ALU.max),
               reads=[self.b_bm2], writes=[self.b_bm2])
        p.step("vector", lambda e: e.tensor_scalar(out=self.shh[:, 0:nb],
               in0=self.bm2[:, 0:nb], scalar1=1.0 / 12.0, scalar2=None,
               op0=ALU.mult),
               reads=[self.b_bm2], writes=[self.b_sh])
        p.step("vector", lambda e: e.reciprocal(self.iv2[:, 0:nb], self.bm2[:, 0:nb]),
               reads=[self.b_bm2], writes=[self.b_iv2])
        p.step("vector", lambda e: e.tensor_tensor(out=self.iv2[:, 0:nb],
               in0=self.iv2[:, 0:nb], in1=self.iv2[:, 0:nb], op=ALU.mult),
               reads=[self.b_iv2], writes=[self.b_iv2])
        p.step(z2_eng, lambda e: e.tensor_tensor(out=r3(x2),
               in0=r3(x2), in1=bc16(self.iv2, nb), op=ALU.mult),
               reads=[bx2, self.b_iv2], writes=[bx2])
        for j in range(4):
            p.step("scalar", lambda e, j=j: e.activation(out=t[j][:, 0:width],
                   in_=x2, func=AF.Sign, bias=self.cst_t[:, j:j + 1]),
                   reads=[bx2, self.cst_b], writes=[bt[j]])
        p.step("vector", lambda e: e.tensor_scalar(out=t[4][:, 0:width],
               in0=x2, scalar1=MID2S[4], scalar2=2.0,
               op0=ALU.is_gt, op1=ALU.mult),
               reads=[bx2], writes=[bt[4]])
        p.step("vector", lambda e: e.tensor_scalar(out=t[5][:, 0:width],
               in0=x2, scalar1=MID2S[5], scalar2=2.0,
               op0=ALU.is_gt, op1=ALU.mult),
               reads=[bx2], writes=[bt[5]])
        p.step("vector", lambda e: e.tensor_tensor(out=t[4][:, 0:width],
               in0=t[4][:, 0:width], in1=t[5][:, 0:width], op=ALU.add),
               reads=[bt[4], bt[5]], writes=[bt[4]])
        p.step("vector", lambda e: e.tensor_scalar(out=t[5][:, 0:width],
               in0=x2, scalar1=MID2S[6], scalar2=4.0,
               op0=ALU.is_gt, op1=ALU.mult),
               reads=[bx2], writes=[bt[5]])
        p.step("vector", lambda e: e.tensor_tensor(out=t[4][:, 0:width],
               in0=t[4][:, 0:width], in1=t[5][:, 0:width], op=ALU.add),
               reads=[bt[4], bt[5]], writes=[bt[4]])
        p.step("vector", lambda e: e.tensor_tensor(out=t[0][:, 0:width],
               in0=t[0][:, 0:width], in1=t[1][:, 0:width], op=ALU.add),
               reads=[bt[0], bt[1]], writes=[bt[0]])
        p.step("vector", lambda e: e.tensor_tensor(out=t[2][:, 0:width],
               in0=t[2][:, 0:width], in1=t[3][:, 0:width], op=ALU.add),
               reads=[bt[2], bt[3]], writes=[bt[2]])
        p.step("vector", lambda e: e.tensor_tensor(out=t[0][:, 0:width],
               in0=t[0][:, 0:width], in1=t[2][:, 0:width], op=ALU.add),
               reads=[bt[0], bt[2]], writes=[bt[0]])
        p.step("vector", lambda e: e.tensor_scalar(out=t[0][:, 0:width],
               in0=t[0][:, 0:width], scalar1=0.5, scalar2=2.0,
               op0=ALU.mult, op1=ALU.add),
               reads=[bt[0]], writes=[bt[0]])
        p.step("vector", lambda e: e.tensor_tensor(out=t[0][:, 0:width],
               in0=t[0][:, 0:width], in1=t[4][:, 0:width], op=ALU.add),
               reads=[bt[0], bt[4]], writes=[bt[0]])
        p.step("vector", lambda e: e.tensor_tensor(out=t[0][:, 0:width],
               in0=t[0][:, 0:width], in1=sg[:, 0:width], op=ALU.mult),
               reads=[bt[0], bsg], writes=[bt[0]])
        p.step(dst_eng, lambda e: e.tensor_tensor(
               out=r3(dst_t[:, dst_off:dst_off + width]),
               in0=r3(t[0][:, 0:width]), in1=bc16(self.shh, nb), op=ALU.mult),
               reads=[bt[0], self.b_sh], writes=[dst_b])


class QuantPair:

    def __init__(self, nc, sb, qw, cst_t, cst_b):
        self.qs = [Quant(nc, sb, qw, cst_t, cst_b) for _ in range(2)]
        self.i = 0
        self.qw = qw

    def emit_multi(self, p, src_t, src_b, dst_t, dst_b, width, src_off=0,
                   dst_off=0, z2_eng="vector", dst_eng="gpsimd"):
        off = 0
        while off < width:
            w = min(self.qw, width - off)
            self.qs[self.i % 2].emit(p, src_t, src_b, dst_t, dst_b, w,
                                     src_off + off, dst_off + off,
                                     z2_eng, dst_eng)
            self.i += 1
            off += w


def emit_ln(p, src_t, src_b, dst_t, dst_b, ln):
    t = lambda n: ln[n + "_t"]
    b = lambda n: ln[n]
    p.step("vector", lambda e: e.bn_stats(t("st")[:, 0:6], src_t[:, 0:512]),
           reads=[src_b], writes=[b("st")])
    p.step("vector", lambda e: e.bn_stats(t("st")[:, 6:12], src_t[:, 512:1024]),
           reads=[src_b], writes=[b("st")])
    p.step("vector", lambda e: e.bn_aggr(t("mv")[:], t("st")[:]),
           reads=[b("st")], writes=[b("mv")])
    p.step("vector", lambda e: e.tensor_scalar(out=t("ve")[:], in0=t("mv")[:, 1:2],
           scalar1=1e-5, scalar2=None, op0=ALU.add),
           reads=[b("mv")], writes=[b("ve")])
    p.step("scalar", lambda e: e.activation(out=t("rs")[:], in_=t("ve")[:],
           func=AF.Sqrt),
           reads=[b("ve")], writes=[b("rs")])
    p.step("vector", lambda e: e.reciprocal(t("rs")[:], t("rs")[:]),
           reads=[b("rs")], writes=[b("rs")])
    p.step("vector", lambda e: e.tensor_scalar(out=dst_t[:], in0=src_t[:],
           scalar1=t("mv")[:, 0:1], scalar2=t("rs")[:, 0:1],
           op0=ALU.subtract, op1=ALU.mult),
           reads=[src_b, b("mv"), b("rs")], writes=[dst_b])


def build_program():
    nc = bass.Bass("TRN2", target_bir_lowering=False)
    x_h = nc.dram_tensor("x", [TOK, DM], F32, kind="ExternalInput").ap()
    wqkvT_h = nc.dram_tensor("wqkvT", [DM, 3 * DM], F32, kind="ExternalInput").ap()
    woT_h = nc.dram_tensor("woT", [DM, DM], F32, kind="ExternalInput").ap()
    wg_h = nc.dram_tensor("wg", [DI, DM], F32, kind="ExternalInput").ap()
    wu_h = nc.dram_tensor("wu", [DI, DM], F32, kind="ExternalInput").ap()
    wd_h = nc.dram_tensor("wd", [DM, DI], F32, kind="ExternalInput").ap()
    ident_h = nc.dram_tensor("ident", [P, P], BF16, kind="ExternalInput").ap()
    maskT_h = nc.dram_tensor("maskT", [P, P], BF16, kind="ExternalInput").ap()
    tot_h = nc.dram_tensor("tot", [P, NCH_SEQ], F32, kind="ExternalInput").ap()
    out_h = nc.dram_tensor("out", [TOK, DM], F32, kind="ExternalOutput").ap()
    x1_d = nc.dram_tensor("x1scr", [TOK, DM], F32, kind="Internal").ap()
    wgq_d = nc.dram_tensor("wgqscr", [DM, DI], BF16, kind="Internal").ap()
    wuq_d = nc.dram_tensor("wuqscr", [DM, DI], BF16, kind="Internal").ap()
    wdq_d = nc.dram_tensor("wdqscr", [DI, DM], BF16, kind="Internal").ap()

    with ExitStack() as st:
        sb = lambda n, s, d=F32: st.enter_context(nc.sbuf_tensor(n, s, d))
        mksem = lambda n: st.enter_context(nc.semaphore(n))

        p = Prog(nc)
        sem_handles = {}
        semnames = ["s_v", "s_a", "s_pe", "s_g"] + ["s_d%d" % i for i in range(NDMASEM)]
        for semn in semnames:
            h = mksem(semn)
            sem_handles[semn] = h
            p.sem(semn, h)
        nums = sorted(h.num for h in sem_handles.values())
        nc.gpsimd.sem_clear(range(nums[0], nums[-1] + 1))
        nc.all_engine_barrier()

        ident_t = sb("identt", [P, P], BF16); ident_b = Buf()
        maskT_t = sb("maskt", [P, P], BF16); maskT_b = Buf()
        invt_t = sb("invt", [P, NCH_SEQ], F32); invt_b = Buf()
        ln = {}
        for n, wdt in (("st", 12), ("mv", 2), ("ve", 1), ("rs", 1)):
            ln[n + "_t"] = sb("ln_" + n, [P, wdt], F32)
            ln[n] = Buf()
        x1d_bufs = [Buf() for _ in range(NCH)]
        cst_t = sb("cstt", [P, 8], F32); cst_b = Buf()
        for i, v in enumerate([-MID2S[0], -MID2S[1], -MID2S[2], -MID2S[3],
                               1.0 / 144.0]):
            p.step("gpsimd", lambda e, i=i, v=v: e.memset(cst_t[:, i:i + 1], v),
                   writes=[cst_b])
        wgq_db, wuq_db, wdq_db = Buf(), Buf(), Buf()

        with ExitStack() as stA:
            sbA = lambda n, s, d=F32: stA.enter_context(nc.sbuf_tensor(n, s, d))
            psA = lambda n, s, d=F32: stA.enter_context(nc.psum_tensor(n, s, d))

            q = QuantPair(nc, sbA, 704, cst_t, cst_b)

            state_t = sbA("state", [P, NHP * 64]); state_b = Buf()
            stateb_t = sbA("stateb", [P, NHP * 64], BF16); stateb_b = Buf()

            p.step("sync", lambda e: e.dma_start(out=ident_t[:], in_=ident_h[:, :]),
                   writes=[ident_b], dma=True)
            p.step("sync", lambda e: e.dma_start(out=maskT_t[:], in_=maskT_h[:, :]),
                   writes=[maskT_b], dma=True)
            p.step("sync", lambda e: e.dma_start(out=invt_t[:], in_=tot_h[:, :]),
                   writes=[invt_b], dma=True)
            p.step("vector", lambda e: e.reciprocal(invt_t[:], invt_t[:]),
                   reads=[invt_b], writes=[invt_b])
            p.step("vector", lambda e: e.tensor_scalar(out=invt_t[:], in0=invt_t[:],
                   scalar1=0.125, scalar2=None, op0=ALU.mult),
                   reads=[invt_b], writes=[invt_b])

            wqkv_t = sbA("wqkvb", [P, NKC, 3 * DM], BF16); wqkv_b = Buf()
            wo_t = sbA("wob", [P, NKC, DM], BF16); wo_b = Buf()
            wstg_t = sbA("wstg", [P, 3 * DM]); wstg_b = Buf()
            for kc in range(NKC):
                p.step("sync", lambda e, kc=kc: e.dma_start(
                    out=wstg_t[:, 0:3 * DM], in_=wqkvT_h[kc * P:(kc + 1) * P, :]),
                    writes=[wstg_b], dma=True)
                p.step("vector", lambda e, kc=kc: e.tensor_copy(wqkv_t[:, kc, :], wstg_t[:, 0:3 * DM]),
                       reads=[wstg_b], writes=[wqkv_b])
            for kc in range(NKC):
                p.step("sync", lambda e, kc=kc: e.dma_start(
                    out=wstg_t[:, 0:DM], in_=woT_h[kc * P:(kc + 1) * P, :]),
                    writes=[wstg_b], dma=True)
                p.step("vector", lambda e, kc=kc: e.tensor_copy(wo_t[:, kc, :], wstg_t[:, 0:DM]),
                       reads=[wstg_b], writes=[wo_b])

            x_t = [sbA("xin%d" % i, [P, DM]) for i in range(3)]
            x_b = [Buf() for _ in range(3)]
            lnb_t = [sbA("lnb%d" % i, [P, DM], BF16) for i in range(2)]
            lnb_b = [Buf(), Buf()]
            lnfm_t = [sbA("lnfm%d" % i, [P, NKC, P], BF16) for i in range(2)]
            lnfm_b = [Buf(), Buf()]
            q_tm_t = [sbA("qtm%d" % i, [P, DM], BF16) for i in range(2)]
            q_tm_b = [Buf(), Buf()]
            k_tm_t = [sbA("ktm%d" % i, [P, DM], BF16) for i in range(2)]
            k_tm_b = [Buf(), Buf()]
            v_tm_t = [sbA("vtm%d" % i, [P, DM], BF16) for i in range(2)]
            v_tm_b = [Buf(), Buf()]
            q_fm_t = [sbA("qfm%d" % i, [P, NKC, P], BF16) for i in range(2)]
            q_fm_b = [Buf(), Buf()]
            k_fm_t = [sbA("kfm%d" % i, [P, NKC, P], BF16) for i in range(2)]
            k_fm_b = [Buf(), Buf()]
            scT_t = sbA("scT", [P, 4, P], BF16); scT_b = [Buf(), Buf()]
            ao_t = sbA("aofm", [P, NKC, P], BF16); ao_b = Buf()
            x1_t = [sbA("x1t%d" % i, [P, DM]) for i in range(2)]
            x1_b = [Buf(), Buf()]
            wpin_t = sbA("wpin", [P, DI]); wpin_b = Buf()
            wq_t = [sbA("wqtile%d" % i, [P, DI], BF16) for i in range(2)]
            wq_b = [Buf(), Buf()]
            wtr_t = [sbA("wtrtile%d" % i, [P, 4, P], BF16) for i in range(2)]
            wtr_b = [Buf(), Buf()]

            pp = [psA("ps_pp%d" % i, [P, 512]) for i in range(2)]
            pp_b = [Buf(), Buf()]
            ps_t = psA("ps_tr", [P, 4, P], BF16); ps_t_b = Buf()
            ps_tB = psA("ps_trB", [P, 4, P], BF16); ps_tB_b = Buf()
            pst = [ps_t, ps_tB]; pstb = [ps_t_b, ps_tB_b]
            ps_scA = psA("ps_scA", [P, P]); ps_scA_b = Buf()
            ps_scB = psA("ps_scB", [P, P]); ps_scB_b = Buf()
            ps_o = psA("ps_o", [P, 4, P]); ps_o_b = [Buf() for _ in range(4)]
            ps_kv = psA("ps_kv", [P, 512]); ps_kv_b = Buf()

            prep_units = []
            for w_h, wq_dst, dstb, nrt, cols in (
                    (wg_h, wgq_d, wgq_db, NKI, DM),
                    (wu_h, wuq_d, wuq_db, NKI, DM)):
                for rt in range(nrt):
                    prep_units.append((w_h, wq_dst, dstb, rt, cols, None))
            for rt in range(NKC):
                for half in range(2):
                    prep_units.append((wd_h, wdq_d, wdq_db, rt, DI, half))

            def emit_prep_q(unit, pb):
                w_h, wq_dst, dstb, rt, cols, half = unit
                if half is None or half == 0:
                    p.step("sync", lambda e: e.dma_start(
                        out=wpin_t[:, 0:cols], in_=w_h[rt * P:(rt + 1) * P, :]),
                        writes=[wpin_b], dma=True)
                if half is None:
                    q.emit_multi(p, wpin_t, wpin_b, wq_t[pb], wq_b[pb], cols,
                                 z2_eng="gpsimd", dst_eng="gpsimd")
                else:
                    q.emit_multi(p, wpin_t, wpin_b, wq_t[pb], wq_b[pb], HI,
                                 src_off=half * HI, dst_off=half * HI,
                                 z2_eng="gpsimd", dst_eng="gpsimd")

            def emit_prep_t_byidx(j):
                emit_prep_t(prep_units[j], j % 2)

            def emit_prep_t(unit, pb):
                w_h, wq_dst, dstb, rt, cols, half = unit
                if half is None:
                    ccs = list(range(cols // P))
                else:
                    ccs = list(range(half * (HI // P), (half + 1) * (HI // P)))
                for g in range((len(ccs) + 3) // 4):
                    grp = ccs[g * 4:(g + 1) * 4]
                    pt, ptb = pst[g % 2], pstb[g % 2]
                    wt, wb = wtr_t[g % 2], wtr_b[g % 2]
                    for i, cc in enumerate(grp):
                        p.step("tensor", lambda e, cc=cc, i=i, pb=pb, pt=pt: e.transpose(
                            pt[:, i, :], wq_t[pb][:, cc * P:(cc + 1) * P], ident_t[:]),
                            reads=[wq_b[pb], ident_b], writes=[ptb])
                    p.step("vector", lambda e, n=len(grp), pt=pt, wt=wt: e.tensor_copy(
                        wt[:, 0:n, :], pt[:, 0:n, :]),
                        reads=[ptb], writes=[wb])
                    for i, cc in enumerate(grp):
                        p.step("sync", lambda e, cc=cc, i=i, wt=wt: e.dma_start(
                            out=wq_dst[cc * P:(cc + 1) * P, rt * P:(rt + 1) * P],
                            in_=wt[:, i, :]),
                            reads=[wb], writes=[dstb], dma=True)

            def a_pre(ct):
                xb = x_b[ct % 3]
                xt = x_t[ct % 3]
                p.step("sync", lambda e, ct=ct, xt=xt: e.dma_start(
                    out=xt[:], in_=x_h[ct * P:(ct + 1) * P, :]),
                    writes=[xb], dma=True)

            def a_prep(ct):
                if not A_PREP:
                    return
                xb, xt = x_b[ct % 3], x_t[ct % 3]
                lb, lt = lnb_b[ct % 2], lnb_t[ct % 2]
                fb, ft = lnfm_b[ct % 2], lnfm_t[ct % 2]
                emit_ln(p, xt, xb, lt, lb, ln)

            def a1(ct):
                cs = ct % NCH_SEQ
                fb, ft = lnfm_b[ct % 2], lnfm_t[ct % 2]
                d = ct % 2
                dp = (ct - 1) % 2
                if ct > 0 and A_TR:
                    bi = 0
                    for src_tt, src_bb, dst_tt, dst_bb in ((q_tm_t[dp], q_tm_b[dp], q_fm_t[dp], q_fm_b[dp]),
                                                           (k_tm_t[dp], k_tm_b[dp], k_fm_t[dp], k_fm_b[dp])):
                        for g in range(2):
                            pt, pb = pst[bi % 2], pstb[bi % 2]
                            bi += 1
                            for i in range(4):
                                cc = g * 4 + i
                                p.step("tensor", lambda e, cc=cc, i=i, src_tt=src_tt, pt=pt: e.transpose(
                                    pt[:, i, :], src_tt[:, cc * P:(cc + 1) * P], ident_t[:]),
                                    reads=[src_bb, ident_b], writes=[pb])
                            p.step("vector", lambda e, g=g, dst_tt=dst_tt, pt=pt: e.tensor_copy(
                                dst_tt[:, g * 4:(g + 1) * 4, :], pt[:]),
                                reads=[pb], writes=[dst_bb])
                if A_PREP:
                    lb2, lt2 = lnb_b[ct % 2], lnb_t[ct % 2]
                    for g in range(2):
                        pt, pb = pst[g % 2], pstb[g % 2]
                        for i in range(4):
                            cc = g * 4 + i
                            p.step("tensor", lambda e, cc=cc, i=i, lt2=lt2, pt=pt: e.transpose(
                                pt[:, i, :], lt2[:, cc * P:(cc + 1) * P], ident_t[:]),
                                reads=[lb2, ident_b], writes=[pb])
                        p.step("vector", lambda e, g=g, ft=ft, pt=pt: e.tensor_copy(
                            ft[:, g * 4:(g + 1) * 4, :], pt[:]),
                            reads=[pb], writes=[fb])
                for slab in range(6 if A_QKV else 0):
                    pg = slab % 2
                    def mm_slab(e, slab=slab, ft=ft, pg=pg):
                        for kc in range(NKC):
                            i = e.matmul(pp[pg][:], ft[:, kc, :],
                                         wqkv_t[:, kc, slab * 512:(slab + 1) * 512],
                                         start=(kc == 0), stop=(kc == NKC - 1))
                        return i
                    p.step("tensor", mm_slab, reads=[fb, wqkv_b], writes=[pp_b[pg]])
                    if slab < 2:
                        p.step("scalar", lambda e, slab=slab, cs=cs, d=d, pg=pg: e.activation(
                            out=q_tm_t[d][:, slab * 512:(slab + 1) * 512],
                            in_=pp[pg][:], func=AF.Copy,
                            scale=invt_t[:, cs:cs + 1]),
                            reads=[pp_b[pg], invt_b], writes=[q_tm_b[d]])
                    elif slab < 4:
                        p.step("scalar", lambda e, slab=slab, d=d, pg=pg: e.activation(
                            out=k_tm_t[d][:, (slab - 2) * 512:(slab - 1) * 512],
                            in_=pp[pg][:], func=AF.Sign),
                            reads=[pp_b[pg]], writes=[k_tm_b[d]])
                    else:
                        p.step("scalar", lambda e, slab=slab, d=d, pg=pg: e.activation(
                            out=v_tm_t[d][:, (slab - 4) * 512:(slab - 3) * 512],
                            in_=pp[pg][:], func=AF.Sign),
                            reads=[pp_b[pg]], writes=[v_tm_b[d]])

            def a2(ct):
                cs = ct % NCH_SEQ
                xb, xt = x_b[ct % 3], x_t[ct % 3]
                d = ct % 2
                qfm, qfmb = q_fm_t[d], q_fm_b[d]
                kfm, kfmb = k_fm_t[d], k_fm_b[d]
                ktm, ktmb = k_tm_t[d], k_tm_b[d]
                vtm, vtmb = v_tm_t[d], v_tm_b[d]
                if cs == 0 and A_ATT:
                    p.step("gpsimd", lambda e: e.memset(state_t[:], 0.0),
                           writes=[state_b])
                    p.step("gpsimd", lambda e: e.memset(stateb_t[:], 0.0),
                           writes=[stateb_b])
                scp = [(ps_scA, ps_scB, [ps_scA_b, ps_scB_b]),
                       (pp[0][:, 0:P], pp[1][:, 0:P], [pp_b[0], pp_b[1]])]

                def emit_sc(hp):
                    par = hp % 2
                    pa, pb, bufs = scp[par]
                    def mm_sc(e, hp=hp, pa=pa, pb=pb):
                        e.matmul(pa[:], kfm[0:64, hp, :], qfm[0:64, hp, :],
                                 start=True, stop=True)
                        return e.matmul(pb[:], kfm[64:128, hp, :],
                                        qfm[64:128, hp, :], start=True, stop=True)
                    p.step("tensor", mm_sc, reads=[kfmb, qfmb], writes=bufs)

                def emit_mask(hp):
                    par = hp % 2
                    pa, pb, bufs = scp[par]
                    p.step("vector", lambda e, par=par, pa=pa: e.tensor_tensor(
                        out=scT_t[:, 2 * par, :], in0=pa[:], in1=maskT_t[:],
                        op=ALU.mult),
                        reads=[bufs[0], maskT_b], writes=[scT_b[par]])
                    p.step("vector", lambda e, par=par, pb=pb: e.tensor_tensor(
                        out=scT_t[:, 2 * par + 1, :], in0=pb[:], in1=maskT_t[:],
                        op=ALU.mult),
                        reads=[bufs[1], maskT_b], writes=[scT_b[par]])

                def emit_att(hp):
                    par = hp % 2
                    def mm_att(e, hp=hp, par=par):
                        e.matmul(ps_o[0:64, hp % 4, :], vtm[:, hp * P:hp * P + 64],
                                 scT_t[:, 2 * par, :], start=True, stop=False)
                        e.matmul(ps_o[0:64, hp % 4, :], stateb_t[0:64, hp * 64:(hp + 1) * 64],
                                 qfm[0:64, hp, :], start=False, stop=True)
                        e.matmul(ps_o[64:128, hp % 4, :], vtm[:, hp * P + 64:hp * P + 128],
                                 scT_t[:, 2 * par + 1, :], start=True, stop=False)
                        return e.matmul(ps_o[64:128, hp % 4, :], stateb_t[64:128, hp * 64:(hp + 1) * 64],
                                        qfm[64:128, hp, :], start=False, stop=True)
                    p.step("tensor", mm_att, reads=[vtmb, scT_b[par], stateb_b, qfmb],
                           writes=[ps_o_b[hp % 4]])

                def emit_ckv(hp):
                    def mm_ckv(e, hp=hp):
                        e.matmul(ps_kv[0:64, hp * 64:(hp + 1) * 64],
                                 ktm[:, hp * P:hp * P + 64],
                                 vtm[:, hp * P:hp * P + 64], start=True, stop=True)
                        return e.matmul(ps_kv[64:128, hp * 64:(hp + 1) * 64],
                                        ktm[:, hp * P + 64:hp * P + 128],
                                        vtm[:, hp * P + 64:hp * P + 128],
                                        start=True, stop=True)
                    p.step("tensor", mm_ckv, reads=[ktmb, vtmb], writes=[ps_kv_b])

                if A_ATT:
                    emit_sc(0)
                    for hp in range(NHP):
                        emit_mask(hp)
                        if hp + 1 < NHP:
                            emit_sc(hp + 1)
                        emit_att(hp)
                        emit_ckv(hp)
                        if hp % 4 == 3:
                            g = hp // 4
                            p.step("scalar", lambda e, g=g: e.copy(
                                out=ao_t[:, g * 4:(g + 1) * 4, :], in_=ps_o[:, 0:4, :]),
                                reads=[ps_o_b[0], ps_o_b[1], ps_o_b[2], ps_o_b[3]],
                                writes=[ao_b])
                    p.step("vector", lambda e: e.tensor_tensor(
                        out=state_t[:], in0=state_t[:], in1=ps_kv[:], op=ALU.add),
                        reads=[ps_kv_b, state_b], writes=[state_b])
                    p.step("vector", lambda e: e.tensor_copy(stateb_t[:], state_t[:]),
                           reads=[state_b], writes=[stateb_b])
                x1b, x1t = x1_b[ct % 2], x1_t[ct % 2]
                for slab in range(2 if A_WO else 0):
                    pg = slab % 2
                    def mm_wo(e, slab=slab, pg=pg):
                        for kc in range(NKC):
                            i = e.matmul(pp[pg][:], ao_t[:, kc, :],
                                         wo_t[:, kc, slab * 512:(slab + 1) * 512],
                                         start=(kc == 0), stop=(kc == NKC - 1))
                        return i
                    p.step("tensor", mm_wo, reads=[ao_b, wo_b], writes=[pp_b[pg]])
                    p.step("vector", lambda e, slab=slab, xt=xt, x1t=x1t, pg=pg: e.tensor_tensor(
                        out=x1t[:, slab * 512:(slab + 1) * 512], in0=pp[pg][:],
                        in1=xt[:, slab * 512:(slab + 1) * 512], op=ALU.add),
                        reads=[pp_b[pg], xb], writes=[x1b])
                p.step("sync", lambda e, ct=ct, x1t=x1t: e.dma_start(
                    out=x1_d[ct * P:(ct + 1) * P, :], in_=x1t[:]),
                    reads=[x1b], writes=[x1d_bufs[ct]], dma=True)

            if A_ON:
                a_pre(0)
                a_pre(1)
                a_prep(0)
                a_pre(2)
                a_prep(1)
                a1(0)
            pu = 0
            pending_t = []
            for ct in range(NCH):
                if PREP_ON:
                    while pending_t:
                        emit_prep_t_byidx(pending_t.pop(0))
                if A_ON:
                    if ct + 1 < NCH:
                        a1(ct + 1)
                    if ct + 2 < NCH:
                        a_prep(ct + 2)
                    a2(ct)
                    if ct + 3 < NCH:
                        a_pre(ct + 3)
                if PREP_ON:
                    while pu < len(prep_units) and pu < (ct + 1) * len(prep_units) // (NCH - 4):
                        emit_prep_q(prep_units[pu], pu % 2)
                        pending_t.append(pu)
                        pu += 1
            if PREP_ON:
                while pu < len(prep_units):
                    while pending_t:
                        emit_prep_t_byidx(pending_t.pop(0))
                    emit_prep_q(prep_units[pu], pu % 2)
                    pending_t.append(pu)
                    pu += 1
                while pending_t:
                    emit_prep_t_byidx(pending_t.pop(0))

            with nc.Block() as blockA:
                p.flush(blockA)

        with ExitStack() as stB:
            sbB = lambda n, s, d=F32: stB.enter_context(nc.sbuf_tensor(n, s, d))
            psB = lambda n, s, d=F32: stB.enter_context(nc.psum_tensor(n, s, d))

            qB = QuantPair(nc, sbB, 576, cst_t, cst_b)

            wgq_t = sbB("wgqb", [P, NKC, DI], BF16)
            wuq_t = sbB("wuqb", [P, NKC, DI], BF16)
            wdq_t = sbB("wdqb", [P, NKI, DM], BF16)
            wgq_bs = [Buf() for _ in range(NKC)]
            wuq_bs = [Buf() for _ in range(NKC)]
            wdq_bs = [Buf() for _ in range(NKI)]
            for kc in range(NKC if B_ON else 0):
                p.step("sync", lambda e, kc=kc: e.dma_start(
                    out=wgq_t[:, kc, :], in_=wgq_d[kc * P:(kc + 1) * P, :]),
                    reads=[wgq_db], writes=[wgq_bs[kc]], dma=True)
                p.step("sync", lambda e, kc=kc: e.dma_start(
                    out=wuq_t[:, kc, :], in_=wuq_d[kc * P:(kc + 1) * P, :]),
                    reads=[wuq_db], writes=[wuq_bs[kc]], dma=True)
            for kc in range(NKI if B_ON else 0):
                p.step("sync", lambda e, kc=kc: e.dma_start(
                    out=wdq_t[:, kc, :], in_=wdq_d[kc * P:(kc + 1) * P, :]),
                    reads=[wdq_db], writes=[wdq_bs[kc]], dma=True)

            x1b_t = [sbB("x1in%d" % i, [P, DM]) for i in range(3)]
            x1b_b = [Buf() for _ in range(3)]
            ln2_t = sbB("ln2", [P, DM]); ln2_b = Buf()
            xq_t = [sbB("xq%d" % i, [P, DM], BF16) for i in range(2)]
            xq_b = [Buf(), Buf()]
            xq_fm_t = [sbB("xqfm%d" % i, [P, NKC, P], BF16) for i in range(2)]
            xq_fm_b = [Buf(), Buf()]
            h_t = [sbB("hhalf%d" % i, [P, HI]) for i in range(2)]
            h_b = [Buf(), Buf()]
            hq_t = [[sbB("hq%d_%d" % (j, i), [P, HI], BF16) for i in range(2)]
                    for j in range(2)]
            hq_b = [[Buf(), Buf()], [Buf(), Buf()]]
            hq_fm_t = [sbB("hqfm%d" % i, [P, NKI, P], BF16) for i in range(2)]
            hq_fm_b = [Buf(), Buf()]
            ps_g = [psB("ps_g%d" % i, [P, 512]) for i in range(2)]
            ps_g_b = [Buf(), Buf()]
            ps_u = [psB("ps_u%d" % i, [P, 512]) for i in range(2)]
            ps_u_b = [Buf(), Buf()]
            ps_d = [psB("ps_d%d" % i, [P, 512]) for i in range(2)]
            ps_d_b = [Buf(), Buf()]
            ps_t2 = psB("ps_tr2", [P, 4, P], BF16); ps_t2_b = Buf()
            ps_t2B = psB("ps_tr2B", [P, 4, P], BF16); ps_t2B_b = Buf()
            pst2 = [ps_t2, ps_t2B]; pst2b = [ps_t2_b, ps_t2B_b]

            GSLABS = [(0, 512), (512, 512), (1024, 384)]

            out_toks = []

            def b_pre(ct):
                xb, xt = x1b_b[ct % 3], x1b_t[ct % 3]
                p.step("sync", lambda e, ct=ct, xt=xt: e.dma_start(
                    out=xt[:], in_=x1_d[ct * P:(ct + 1) * P, :]),
                    reads=[x1d_bufs[ct]], writes=[xb], dma=True)

            def b_prep(ct):
                xb, xt = x1b_b[ct % 3], x1b_t[ct % 3]
                emit_ln(p, xt, xb, ln2_t, ln2_b, ln)
                xqb, xqt = xq_b[ct % 2], xq_t[ct % 2]
                qB.emit_multi(p, ln2_t, ln2_b, xqt, xqb, DM,
                              z2_eng="vector", dst_eng="vector")

            def b1t(ct):
                fb, ft = xq_fm_b[ct % 2], xq_fm_t[ct % 2]
                xqb, xqt = xq_b[ct % 2], xq_t[ct % 2]
                for g in range(2):
                    pt, pb = pst2[g % 2], pst2b[g % 2]
                    for i in range(4):
                        cc = g * 4 + i
                        p.step("tensor", lambda e, cc=cc, i=i, xqt=xqt, pt=pt: e.transpose(
                            pt[:, i, :], xqt[:, cc * P:(cc + 1) * P], ident_t[:]),
                            reads=[xqb, ident_b], writes=[pb])
                    p.step("scalar", lambda e, g=g, ft=ft, pt=pt: e.copy(
                        out=ft[:, g * 4:(g + 1) * 4, :], in_=pt[:]),
                        reads=[pb], writes=[fb])

            def b1(ct):
                fb, ft = xq_fm_b[ct % 2], xq_fm_t[ct % 2]
                d = ct % 2
                hfmt, hfmb = hq_fm_t[d], hq_fm_b[d]
                for half in range(2):
                    hb, ht = h_b[half], h_t[half]
                    hqb, hqt = hq_b[d][half], hq_t[d][half]
                    for si, (off, wl) in enumerate(GSLABS):
                        pg = si % 2

                        def mm_g(e, half=half, ft=ft, off=off, wl=wl, pg=pg):
                            for kc in range(NKC):
                                i = e.matmul(ps_g[pg][:, 0:wl], ft[:, kc, :],
                                             wgq_t[:, kc, half * HI + off:half * HI + off + wl],
                                             start=(kc == 0), stop=(kc == NKC - 1))
                            return i
                        p.step("tensor", mm_g, reads=[fb] + wgq_bs, writes=[ps_g_b[pg]])

                        def mm_u(e, half=half, ft=ft, off=off, wl=wl, pg=pg):
                            for kc in range(NKC):
                                i = e.matmul(ps_u[pg][:, 0:wl], ft[:, kc, :],
                                             wuq_t[:, kc, half * HI + off:half * HI + off + wl],
                                             start=(kc == 0), stop=(kc == NKC - 1))
                            return i
                        p.step("tensor", mm_u, reads=[fb] + wuq_bs, writes=[ps_u_b[pg]])
                        p.step("scalar", lambda e, ht=ht, off=off, wl=wl, pg=pg: e.activation(
                            out=ht[:, off:off + wl], in_=ps_g[pg][:, 0:wl], func=AF.Silu),
                            reads=[ps_g_b[pg]], writes=[hb])
                        p.step("vector", lambda e, ht=ht, off=off, wl=wl, pg=pg: e.tensor_tensor(
                            out=ht[:, off:off + wl], in0=ps_u[pg][:, 0:wl],
                            in1=ht[:, off:off + wl], op=ALU.mult),
                            reads=[ps_u_b[pg], hb], writes=[hb])
                    qB.emit_multi(p, ht, hb, hqt, hqb, HI,
                                  z2_eng="gpsimd", dst_eng="gpsimd")

            def b2(ct):
                xb, xt = x1b_b[ct % 3], x1b_t[ct % 3]
                d = ct % 2
                hfmt, hfmb = hq_fm_t[d], hq_fm_b[d]
                bi = 0
                for half in range(2):
                    hqb, hqt = hq_b[d][half], hq_t[d][half]
                    ccs = list(range(11))
                    for g in range(3):
                        grp = ccs[g * 4:(g + 1) * 4]
                        if not grp:
                            break
                        pt, pb = pst2[bi % 2], pst2b[bi % 2]
                        bi += 1
                        for i, cc in enumerate(grp):
                            p.step("tensor", lambda e, cc=cc, i=i, hqt=hqt, pt=pt: e.transpose(
                                pt[:, i, :], hqt[:, cc * P:(cc + 1) * P], ident_t[:]),
                                reads=[hqb, ident_b], writes=[pb])
                        p.step("scalar", lambda e, g=g, grp=grp, half=half, hfmt=hfmt, pt=pt: e.copy(
                            out=hfmt[:, half * 11 + grp[0]:half * 11 + grp[0] + len(grp), :],
                            in_=pt[:, 0:len(grp), :]),
                            reads=[pb], writes=[hfmb])
                for slab in range(2):
                    pg = slab % 2
                    def mm_d(e, slab=slab, hfmt=hfmt, pg=pg):
                        for kc in range(NKI):
                            i = e.matmul(ps_d[pg][:], hfmt[:, kc, :],
                                         wdq_t[:, kc, slab * 512:(slab + 1) * 512],
                                         start=(kc == 0), stop=(kc == NKI - 1))
                        return i
                    p.step("tensor", mm_d, reads=[hfmb] + wdq_bs, writes=[ps_d_b[pg]])
                    p.step("vector", lambda e, slab=slab, xt=xt, pg=pg: e.tensor_tensor(
                        out=xt[:, slab * 512:(slab + 1) * 512], in0=ps_d[pg][:],
                        in1=xt[:, slab * 512:(slab + 1) * 512], op=ALU.add),
                        reads=[ps_d_b[pg], xb], writes=[xb])
                out_toks.append(p.step("sync", lambda e, ct=ct, xt=xt: e.dma_start(
                    out=out_h[ct * P:(ct + 1) * P, :], in_=xt[:]),
                    reads=[xb], dma=True))

            if B_ON:
                b_pre(0)
                b_pre(1)
                b_prep(0)
                b_pre(2)
                b_prep(1)
                b1t(0)
                b1(0)
                for ct in range(NCH):
                    if ct + 1 < NCH:
                        b1t(ct + 1)
                    if ct + 2 < NCH:
                        b_prep(ct + 2)
                    if ct + 1 < NCH:
                        b1(ct + 1)
                    b2(ct)
                    if ct + 3 < NCH:
                        b_pre(ct + 3)

            p.step("gpsimd", lambda e: e.memset(ln["st_t"][:, 0:1], 0.0),
                   extra=out_toks, writes=[ln["st"]])

            with nc.Block() as blockB:
                p.flush(blockB)
    return nc


_NC_CACHE = {}


def kernel(L0_attn_x, L0_attn_W_qkv, L0_attn_W_o, L0_ffn_w_gate, L0_ffn_w_up,
           L0_ffn_w_down):
    x = np.ascontiguousarray(np.asarray(L0_attn_x, dtype=np.float32))
    wqkvT = np.ascontiguousarray(np.asarray(L0_attn_W_qkv, dtype=np.float32).T)
    woT = np.ascontiguousarray(np.asarray(L0_attn_W_o, dtype=np.float32).T)
    wg = np.ascontiguousarray(np.asarray(L0_ffn_w_gate, dtype=np.float32))
    wu = np.ascontiguousarray(np.asarray(L0_ffn_w_up, dtype=np.float32))
    wd = np.ascontiguousarray(np.asarray(L0_ffn_w_down, dtype=np.float32))
    ident = np.eye(P, dtype=np.float32).astype(ml_dtypes.bfloat16)
    maskT = np.triu(np.ones((P, P), dtype=np.float32)).astype(ml_dtypes.bfloat16)
    tot = (np.arange(1, T + 1, dtype=np.float32).reshape(NCH_SEQ, P).T).copy()

    if "nc" not in _NC_CACHE:
        _NC_CACHE["nc"] = build_program()
    nc = _NC_CACHE["nc"]

    in_maps = []
    for c in range(NCORES):
        xs = np.ascontiguousarray(
            x[c * SEQ_PER_CORE:(c + 1) * SEQ_PER_CORE].reshape(TOK, DM))
        in_maps.append({"x": xs, "wqkvT": wqkvT, "woT": woT, "wg": wg,
                        "wu": wu, "wd": wd, "ident": ident, "maskT": maskT,
                        "tot": tot})
    res = run_bass_kernel_spmd(nc, in_maps, core_ids=list(range(NCORES)))
    out = np.concatenate([np.asarray(r["out"]) for r in res.results], axis=0)
    return out.reshape(B_FULL, T, DM).astype(np.float32)


# version 13 (final)
# speedup vs baseline: 1.1442x; 1.0770x over previous
import numpy as np
import ml_dtypes
import concourse.bass as bass
import concourse.mybir as mybir
from concourse.bass_utils import run_bass_kernel_spmd
from contextlib import ExitStack

F32 = mybir.dt.float32
BF16 = mybir.dt.bfloat16
AF = mybir.ActivationFunctionType
ALU = mybir.AluOpType
AX = mybir.AxisListType

P = 128
B_FULL, T, DM = 16, 4096, 1024
NCORES = 8
SEQ_PER_CORE = B_FULL // NCORES
TOK = SEQ_PER_CORE * T
NCH_SEQ = T // P
NCH = TOK // P
DI = 2816
HI = DI // 2
NKC = DM // P
NKI = DI // P
NHP = 8

MID2S = [(m * m) / 36.0 for m in (0.25, 0.75, 1.25, 1.75, 2.5, 3.5, 5.0)]

ENGS = ("sync", "vector", "scalar", "tensor", "gpsimd")
ENG_SEM = {"sync": "s_dma", "vector": "s_v", "scalar": "s_a",
           "tensor": "s_pe", "gpsimd": "s_g"}
NDMASEM = 8
PREP_ON = True
A_ON = True
B_ON = True
A_PREP = True
A_QKV = True
A_TR = True
A_ATT = True
A_WO = True
A_ATT_LVL = 6


class Buf:
    __slots__ = ("w", "r")

    def __init__(self):
        self.w = None
        self.r = []


class Prog:
    def __init__(self, nc):
        self.nc = nc
        self.steps = {e: [] for e in ENGS}
        self.sems = {}
        self.cnt = {}
        self.dma_idx = 0

    def sem(self, name, handle):
        self.sems[name] = handle
        self.cnt[name] = 0

    def step(self, eng, emit, reads=(), writes=(), extra=(), dma=False):
        deps = []
        for b in reads:
            if b.w is not None:
                deps.append(b.w)
        for b in writes:
            if b.w is not None:
                deps.append(b.w)
            deps.extend(b.r)
        deps.extend(extra)
        if dma:
            idx = self.dma_idx
            self.dma_idx += 1
            semn = "s_d%d" % (idx % NDMASEM)
            self.cnt[semn] += 16
            if self.cnt[semn] > 16:
                deps.append((eng, semn, self.cnt[semn] - 16, True))
            tok = (eng, semn, self.cnt[semn], False)
            self.steps[eng].append((tuple(deps), emit, semn, 16))
        else:
            semn = ENG_SEM[eng]
            self.cnt[semn] += 1
            tok = (eng, semn, self.cnt[semn], False)
            self.steps[eng].append((tuple(deps), emit, semn, 1))
        for b in reads:
            b.r.append(tok)
        for b in writes:
            b.w = tok
            b.r = []
        return tok

    def flush(self, block):
        if not hasattr(self, "waited"):
            self.waited = {e: dict.fromkeys(self.sems, 0) for e in ENGS}

        def mk(eng_name, steps):
            def body(eng):
                w = self.waited[eng_name]
                for deps, emit, semn, inc in steps:
                    for (peng, psem, pcnt, psync) in deps:
                        if w[psem] >= pcnt:
                            continue
                        eng.wait_ge(self.sems[psem], pcnt)
                        w[psem] = pcnt
                    inst = emit(eng)
                    inst.then_inc(self.sems[semn], inc)
            return body

        for e in ENGS:
            if self.steps[e]:
                getattr(block, e)(mk(e, self.steps[e]))
        self.steps = {e: [] for e in ENGS}


def r3(t, k=16):
    return t.rearrange("p (b k) -> p b k", k=k)


def bc16(t, nb):
    return t[:, 0:nb].unsqueeze(2).broadcast_to([P, nb, 16])


class Quant:

    _uid = [0]

    def __init__(self, nc, sb, qw, cst_t, cst_b):
        self.qw = qw
        self.cst_t = cst_t
        self.cst_b = cst_b
        nbw = qw // 16
        u = "%d" % Quant._uid[0]
        Quant._uid[0] += 1
        self.sg = sb("q_sg" + u, [P, qw], BF16)
        self.t = [sb("q_t%d_" % i + u, [P, qw], BF16) for i in range(6)]
        self.bm2 = sb("q_bm2" + u, [P, nbw], F32)
        self.iv2 = sb("q_iv2" + u, [P, nbw], F32)
        self.shh = sb("q_sh" + u, [P, nbw], BF16)
        self.b_sg = Buf()
        self.b_t = [Buf() for _ in range(6)]
        self.b_bm2 = Buf()
        self.b_iv2 = Buf()
        self.b_sh = Buf()

    def emit_multi(self, p, src_t, src_b, dst_t, dst_b, width, src_off=0,
                   dst_off=0, z2_eng="vector", dst_eng="gpsimd"):
        off = 0
        while off < width:
            w = min(self.qw, width - off)
            self.emit(p, src_t, src_b, dst_t, dst_b, w, src_off + off,
                      dst_off + off, z2_eng, dst_eng)
            off += w


    def emit(self, p, src_t, src_b, dst_t, dst_b, width, src_off=0, dst_off=0,
             z2_eng="vector", dst_eng="gpsimd"):
        nb = width // 16
        sg, t = self.sg, self.t
        bsg, bt = self.b_sg, self.b_t
        x2 = src_t[:, src_off:src_off + width]
        bx2 = src_b
        p.step("vector", lambda e: e.tensor_reduce(out=self.bm2[:, 0:nb],
               in_=r3(x2), axis=AX.X, op=ALU.max, apply_absolute_value=True),
               reads=[src_b], writes=[self.b_bm2])
        p.step("scalar", lambda e: e.activation(out=sg[:, 0:width], in_=x2,
                                                func=AF.Sign),
               reads=[src_b], writes=[bsg])
        p.step("scalar", lambda e: e.activation(out=x2, in_=x2, func=AF.Square),
               reads=[src_b], writes=[bx2])
        p.step("vector", lambda e: e.tensor_scalar(out=self.bm2[:, 0:nb],
               in0=self.bm2[:, 0:nb], scalar1=6e-12, scalar2=None, op0=ALU.max),
               reads=[self.b_bm2], writes=[self.b_bm2])
        p.step("vector", lambda e: e.tensor_scalar(out=self.shh[:, 0:nb],
               in0=self.bm2[:, 0:nb], scalar1=1.0 / 12.0, scalar2=None,
               op0=ALU.mult),
               reads=[self.b_bm2], writes=[self.b_sh])
        p.step("vector", lambda e: e.reciprocal(self.iv2[:, 0:nb], self.bm2[:, 0:nb]),
               reads=[self.b_bm2], writes=[self.b_iv2])
        p.step("vector", lambda e: e.tensor_tensor(out=self.iv2[:, 0:nb],
               in0=self.iv2[:, 0:nb], in1=self.iv2[:, 0:nb], op=ALU.mult),
               reads=[self.b_iv2], writes=[self.b_iv2])
        p.step(z2_eng, lambda e: e.tensor_tensor(out=r3(x2),
               in0=r3(x2), in1=bc16(self.iv2, nb), op=ALU.mult),
               reads=[bx2, self.b_iv2], writes=[bx2])
        for j in range(4):
            p.step("scalar", lambda e, j=j: e.activation(out=t[j][:, 0:width],
                   in_=x2, func=AF.Sign, bias=self.cst_t[:, j:j + 1]),
                   reads=[bx2, self.cst_b], writes=[bt[j]])
        p.step("vector", lambda e: e.tensor_scalar(out=t[4][:, 0:width],
               in0=x2, scalar1=MID2S[4], scalar2=2.0,
               op0=ALU.is_gt, op1=ALU.mult),
               reads=[bx2], writes=[bt[4]])
        p.step("vector", lambda e: e.tensor_scalar(out=t[5][:, 0:width],
               in0=x2, scalar1=MID2S[5], scalar2=2.0,
               op0=ALU.is_gt, op1=ALU.mult),
               reads=[bx2], writes=[bt[5]])
        p.step("vector", lambda e: e.tensor_tensor(out=t[4][:, 0:width],
               in0=t[4][:, 0:width], in1=t[5][:, 0:width], op=ALU.add),
               reads=[bt[4], bt[5]], writes=[bt[4]])
        p.step("vector", lambda e: e.tensor_scalar(out=t[5][:, 0:width],
               in0=x2, scalar1=MID2S[6], scalar2=4.0,
               op0=ALU.is_gt, op1=ALU.mult),
               reads=[bx2], writes=[bt[5]])
        p.step("vector", lambda e: e.tensor_tensor(out=t[4][:, 0:width],
               in0=t[4][:, 0:width], in1=t[5][:, 0:width], op=ALU.add),
               reads=[bt[4], bt[5]], writes=[bt[4]])
        p.step("vector", lambda e: e.tensor_tensor(out=t[0][:, 0:width],
               in0=t[0][:, 0:width], in1=t[1][:, 0:width], op=ALU.add),
               reads=[bt[0], bt[1]], writes=[bt[0]])
        p.step("vector", lambda e: e.tensor_tensor(out=t[2][:, 0:width],
               in0=t[2][:, 0:width], in1=t[3][:, 0:width], op=ALU.add),
               reads=[bt[2], bt[3]], writes=[bt[2]])
        p.step("vector", lambda e: e.tensor_tensor(out=t[0][:, 0:width],
               in0=t[0][:, 0:width], in1=t[2][:, 0:width], op=ALU.add),
               reads=[bt[0], bt[2]], writes=[bt[0]])
        p.step("vector", lambda e: e.tensor_scalar(out=t[0][:, 0:width],
               in0=t[0][:, 0:width], scalar1=0.5, scalar2=2.0,
               op0=ALU.mult, op1=ALU.add),
               reads=[bt[0]], writes=[bt[0]])
        p.step("vector", lambda e: e.tensor_tensor(out=t[0][:, 0:width],
               in0=t[0][:, 0:width], in1=t[4][:, 0:width], op=ALU.add),
               reads=[bt[0], bt[4]], writes=[bt[0]])
        p.step("vector", lambda e: e.tensor_tensor(out=t[0][:, 0:width],
               in0=t[0][:, 0:width], in1=sg[:, 0:width], op=ALU.mult),
               reads=[bt[0], bsg], writes=[bt[0]])
        p.step(dst_eng, lambda e: e.tensor_tensor(
               out=r3(dst_t[:, dst_off:dst_off + width]),
               in0=r3(t[0][:, 0:width]), in1=bc16(self.shh, nb), op=ALU.mult),
               reads=[bt[0], self.b_sh], writes=[dst_b])


class QuantPair:

    def __init__(self, nc, sb, qw, cst_t, cst_b):
        self.qs = [Quant(nc, sb, qw, cst_t, cst_b) for _ in range(2)]
        self.i = 0
        self.qw = qw

    def emit_multi(self, p, src_t, src_b, dst_t, dst_b, width, src_off=0,
                   dst_off=0, z2_eng="vector", dst_eng="gpsimd"):
        off = 0
        while off < width:
            w = min(self.qw, width - off)
            self.qs[self.i % 2].emit(p, src_t, src_b, dst_t, dst_b, w,
                                     src_off + off, dst_off + off,
                                     z2_eng, dst_eng)
            self.i += 1
            off += w


def emit_ln(p, src_t, src_b, dst_t, dst_b, ln):
    t = lambda n: ln[n + "_t"]
    b = lambda n: ln[n]
    p.step("vector", lambda e: e.bn_stats(t("st")[:, 0:6], src_t[:, 0:512]),
           reads=[src_b], writes=[b("st")])
    p.step("vector", lambda e: e.bn_stats(t("st")[:, 6:12], src_t[:, 512:1024]),
           reads=[src_b], writes=[b("st")])
    p.step("vector", lambda e: e.bn_aggr(t("mv")[:], t("st")[:]),
           reads=[b("st")], writes=[b("mv")])
    p.step("vector", lambda e: e.tensor_scalar(out=t("ve")[:], in0=t("mv")[:, 1:2],
           scalar1=1e-5, scalar2=None, op0=ALU.add),
           reads=[b("mv")], writes=[b("ve")])
    p.step("scalar", lambda e: e.activation(out=t("rs")[:], in_=t("ve")[:],
           func=AF.Sqrt),
           reads=[b("ve")], writes=[b("rs")])
    p.step("vector", lambda e: e.reciprocal(t("rs")[:], t("rs")[:]),
           reads=[b("rs")], writes=[b("rs")])
    p.step("vector", lambda e: e.tensor_scalar(out=dst_t[:], in0=src_t[:],
           scalar1=t("mv")[:, 0:1], scalar2=t("rs")[:, 0:1],
           op0=ALU.subtract, op1=ALU.mult),
           reads=[src_b, b("mv"), b("rs")], writes=[dst_b])


def build_program():
    nc = bass.Bass("TRN2", target_bir_lowering=False)
    x_h = nc.dram_tensor("x", [TOK, DM], F32, kind="ExternalInput").ap()
    wqkvT_h = nc.dram_tensor("wqkvT", [DM, 3 * DM], F32, kind="ExternalInput").ap()
    woT_h = nc.dram_tensor("woT", [DM, DM], F32, kind="ExternalInput").ap()
    wg_h = nc.dram_tensor("wg", [DI, DM], F32, kind="ExternalInput").ap()
    wu_h = nc.dram_tensor("wu", [DI, DM], F32, kind="ExternalInput").ap()
    wd_h = nc.dram_tensor("wd", [DM, DI], F32, kind="ExternalInput").ap()
    ident_h = nc.dram_tensor("ident", [P, P], BF16, kind="ExternalInput").ap()
    maskT_h = nc.dram_tensor("maskT", [P, P], BF16, kind="ExternalInput").ap()
    tot_h = nc.dram_tensor("tot", [P, NCH_SEQ], F32, kind="ExternalInput").ap()
    out_h = nc.dram_tensor("out", [TOK, DM], F32, kind="ExternalOutput").ap()
    x1_d = nc.dram_tensor("x1scr", [TOK, DM], F32, kind="Internal").ap()
    wgq_d = nc.dram_tensor("wgqscr", [DM, DI], BF16, kind="Internal").ap()
    wuq_d = nc.dram_tensor("wuqscr", [DM, DI], BF16, kind="Internal").ap()
    wdq_d = nc.dram_tensor("wdqscr", [DI, DM], BF16, kind="Internal").ap()

    with ExitStack() as st:
        sb = lambda n, s, d=F32: st.enter_context(nc.sbuf_tensor(n, s, d))
        mksem = lambda n: st.enter_context(nc.semaphore(n))

        p = Prog(nc)
        sem_handles = {}
        semnames = ["s_v", "s_a", "s_pe", "s_g"] + ["s_d%d" % i for i in range(NDMASEM)]
        for semn in semnames:
            h = mksem(semn)
            sem_handles[semn] = h
            p.sem(semn, h)
        nums = sorted(h.num for h in sem_handles.values())
        nc.gpsimd.sem_clear(range(nums[0], nums[-1] + 1))
        nc.all_engine_barrier()

        ident_t = sb("identt", [P, P], BF16); ident_b = Buf()
        maskT_t = sb("maskt", [P, P], BF16); maskT_b = Buf()
        invt_t = sb("invt", [P, NCH_SEQ], F32); invt_b = Buf()
        ln = {}
        for n, wdt in (("st", 12), ("mv", 2), ("ve", 1), ("rs", 1)):
            ln[n + "_t"] = sb("ln_" + n, [P, wdt], F32)
            ln[n] = Buf()
        x1d_bufs = [Buf() for _ in range(NCH)]
        cst_t = sb("cstt", [P, 8], F32); cst_b = Buf()
        for i, v in enumerate([-MID2S[0], -MID2S[1], -MID2S[2], -MID2S[3],
                               1.0 / 144.0]):
            p.step("gpsimd", lambda e, i=i, v=v: e.memset(cst_t[:, i:i + 1], v),
                   writes=[cst_b])
        wgq_db, wuq_db, wdq_db = Buf(), Buf(), Buf()

        with ExitStack() as stA:
            sbA = lambda n, s, d=F32: stA.enter_context(nc.sbuf_tensor(n, s, d))
            psA = lambda n, s, d=F32: stA.enter_context(nc.psum_tensor(n, s, d))

            q = QuantPair(nc, sbA, 704, cst_t, cst_b)

            state_t = sbA("state", [P, NHP * 64]); state_b = Buf()
            stateb_t = sbA("stateb", [P, NHP * 64], BF16); stateb_b = Buf()

            p.step("sync", lambda e: e.dma_start(out=ident_t[:], in_=ident_h[:, :]),
                   writes=[ident_b], dma=True)
            p.step("sync", lambda e: e.dma_start(out=maskT_t[:], in_=maskT_h[:, :]),
                   writes=[maskT_b], dma=True)
            p.step("sync", lambda e: e.dma_start(out=invt_t[:], in_=tot_h[:, :]),
                   writes=[invt_b], dma=True)
            p.step("vector", lambda e: e.reciprocal(invt_t[:], invt_t[:]),
                   reads=[invt_b], writes=[invt_b])
            p.step("vector", lambda e: e.tensor_scalar(out=invt_t[:], in0=invt_t[:],
                   scalar1=0.125, scalar2=None, op0=ALU.mult),
                   reads=[invt_b], writes=[invt_b])

            wqkv_t = sbA("wqkvb", [P, NKC, 3 * DM], BF16); wqkv_b = Buf()
            wo_t = sbA("wob", [P, NKC, DM], BF16); wo_b = Buf()
            wstg_t = sbA("wstg", [P, 3 * DM]); wstg_b = Buf()
            for kc in range(NKC):
                p.step("sync", lambda e, kc=kc: e.dma_start(
                    out=wstg_t[:, 0:3 * DM], in_=wqkvT_h[kc * P:(kc + 1) * P, :]),
                    writes=[wstg_b], dma=True)
                p.step("vector", lambda e, kc=kc: e.tensor_copy(wqkv_t[:, kc, :], wstg_t[:, 0:3 * DM]),
                       reads=[wstg_b], writes=[wqkv_b])
            for kc in range(NKC):
                p.step("sync", lambda e, kc=kc: e.dma_start(
                    out=wstg_t[:, 0:DM], in_=woT_h[kc * P:(kc + 1) * P, :]),
                    writes=[wstg_b], dma=True)
                p.step("vector", lambda e, kc=kc: e.tensor_copy(wo_t[:, kc, :], wstg_t[:, 0:DM]),
                       reads=[wstg_b], writes=[wo_b])

            x_t = [sbA("xin%d" % i, [P, DM]) for i in range(3)]
            x_b = [Buf() for _ in range(3)]
            lnb_t = [sbA("lnb%d" % i, [P, DM], BF16) for i in range(2)]
            lnb_b = [Buf(), Buf()]
            lnfm_t = [sbA("lnfm%d" % i, [P, NKC, P], BF16) for i in range(2)]
            lnfm_b = [Buf(), Buf()]
            q_tm_t = [sbA("qtm%d" % i, [P, DM], BF16) for i in range(2)]
            q_tm_b = [Buf(), Buf()]
            k_tm_t = [sbA("ktm%d" % i, [P, DM], BF16) for i in range(2)]
            k_tm_b = [Buf(), Buf()]
            v_tm_t = [sbA("vtm%d" % i, [P, DM], BF16) for i in range(2)]
            v_tm_b = [Buf(), Buf()]
            q_fm_t = [sbA("qfm%d" % i, [P, NKC, P], BF16) for i in range(2)]
            q_fm_b = [Buf(), Buf()]
            k_fm_t = [sbA("kfm%d" % i, [P, NKC, P], BF16) for i in range(2)]
            k_fm_b = [Buf(), Buf()]
            scT_t = sbA("scT", [P, 4, P], BF16); scT_b = [Buf(), Buf()]
            ao_t = sbA("aofm", [P, NKC, P], BF16); ao_b = Buf()
            x1_t = [sbA("x1t%d" % i, [P, DM]) for i in range(2)]
            x1_b = [Buf(), Buf()]
            wpin_t = sbA("wpin", [P, DI]); wpin_b = Buf()
            wq_t = [sbA("wqtile%d" % i, [P, DI], BF16) for i in range(2)]
            wq_b = [Buf(), Buf()]
            wtr_t = [sbA("wtrtile%d" % i, [P, 4, P], BF16) for i in range(2)]
            wtr_b = [Buf(), Buf()]

            pp = [psA("ps_pp%d" % i, [P, 512]) for i in range(2)]
            pp_b = [Buf(), Buf()]
            ps_t = psA("ps_tr", [P, 4, P], BF16); ps_t_b = Buf()
            ps_tB = psA("ps_trB", [P, 4, P], BF16); ps_tB_b = Buf()
            pst = [ps_t, ps_tB]; pstb = [ps_t_b, ps_tB_b]
            ps_scA = psA("ps_scA", [P, P]); ps_scA_b = Buf()
            ps_scB = psA("ps_scB", [P, P]); ps_scB_b = Buf()
            ps_o = psA("ps_o", [P, 4, P]); ps_o_b = [Buf() for _ in range(4)]
            ps_kv = psA("ps_kv", [P, 512]); ps_kv_b = Buf()

            prep_units = []
            for w_h, wq_dst, dstb, nrt, cols in (
                    (wg_h, wgq_d, wgq_db, NKI, DM),
                    (wu_h, wuq_d, wuq_db, NKI, DM)):
                for rt in range(nrt):
                    prep_units.append((w_h, wq_dst, dstb, rt, cols, None))
            for rt in range(NKC):
                for half in range(2):
                    prep_units.append((wd_h, wdq_d, wdq_db, rt, DI, half))

            def emit_prep_q(unit, pb):
                w_h, wq_dst, dstb, rt, cols, half = unit
                if half is None or half == 0:
                    p.step("sync", lambda e: e.dma_start(
                        out=wpin_t[:, 0:cols], in_=w_h[rt * P:(rt + 1) * P, :]),
                        writes=[wpin_b], dma=True)
                if half is None:
                    q.emit_multi(p, wpin_t, wpin_b, wq_t[pb], wq_b[pb], cols,
                                 z2_eng="gpsimd", dst_eng="gpsimd")
                else:
                    q.emit_multi(p, wpin_t, wpin_b, wq_t[pb], wq_b[pb], HI,
                                 src_off=half * HI, dst_off=half * HI,
                                 z2_eng="gpsimd", dst_eng="gpsimd")

            def emit_prep_t_byidx(j):
                emit_prep_t(prep_units[j], j % 2)

            def emit_prep_t(unit, pb):
                w_h, wq_dst, dstb, rt, cols, half = unit
                if half is None:
                    ccs = list(range(cols // P))
                else:
                    ccs = list(range(half * (HI // P), (half + 1) * (HI // P)))
                for g in range((len(ccs) + 3) // 4):
                    grp = ccs[g * 4:(g + 1) * 4]
                    pt, ptb = pst[g % 2], pstb[g % 2]
                    wt, wb = wtr_t[g % 2], wtr_b[g % 2]
                    for i, cc in enumerate(grp):
                        p.step("tensor", lambda e, cc=cc, i=i, pb=pb, pt=pt: e.transpose(
                            pt[:, i, :], wq_t[pb][:, cc * P:(cc + 1) * P], ident_t[:]),
                            reads=[wq_b[pb], ident_b], writes=[ptb])
                    p.step("vector", lambda e, n=len(grp), pt=pt, wt=wt: e.tensor_copy(
                        wt[:, 0:n, :], pt[:, 0:n, :]),
                        reads=[ptb], writes=[wb])
                    for i, cc in enumerate(grp):
                        p.step("sync", lambda e, cc=cc, i=i, wt=wt: e.dma_start(
                            out=wq_dst[cc * P:(cc + 1) * P, rt * P:(rt + 1) * P],
                            in_=wt[:, i, :]),
                            reads=[wb], writes=[dstb], dma=True)

            def a_pre(ct):
                xb = x_b[ct % 3]
                xt = x_t[ct % 3]
                p.step("sync", lambda e, ct=ct, xt=xt: e.dma_start(
                    out=xt[:], in_=x_h[ct * P:(ct + 1) * P, :]),
                    writes=[xb], dma=True)

            def a_prep(ct):
                if not A_PREP:
                    return
                xb, xt = x_b[ct % 3], x_t[ct % 3]
                lb, lt = lnb_b[ct % 2], lnb_t[ct % 2]
                fb, ft = lnfm_b[ct % 2], lnfm_t[ct % 2]
                emit_ln(p, xt, xb, lt, lb, ln)

            def a1(ct):
                cs = ct % NCH_SEQ
                fb, ft = lnfm_b[ct % 2], lnfm_t[ct % 2]
                d = ct % 2
                dp = (ct - 1) % 2
                if ct > 0 and A_TR:
                    bi = 0
                    for src_tt, src_bb, dst_tt, dst_bb in ((q_tm_t[dp], q_tm_b[dp], q_fm_t[dp], q_fm_b[dp]),
                                                           (k_tm_t[dp], k_tm_b[dp], k_fm_t[dp], k_fm_b[dp])):
                        for g in range(2):
                            pt, pb = pst[bi % 2], pstb[bi % 2]
                            bi += 1
                            for i in range(4):
                                cc = g * 4 + i
                                p.step("tensor", lambda e, cc=cc, i=i, src_tt=src_tt, pt=pt: e.transpose(
                                    pt[:, i, :], src_tt[:, cc * P:(cc + 1) * P], ident_t[:]),
                                    reads=[src_bb, ident_b], writes=[pb])
                            p.step("vector", lambda e, g=g, dst_tt=dst_tt, pt=pt: e.tensor_copy(
                                dst_tt[:, g * 4:(g + 1) * 4, :], pt[:]),
                                reads=[pb], writes=[dst_bb])
                if A_PREP:
                    lb2, lt2 = lnb_b[ct % 2], lnb_t[ct % 2]
                    for g in range(2):
                        pt, pb = pst[g % 2], pstb[g % 2]
                        for i in range(4):
                            cc = g * 4 + i
                            p.step("tensor", lambda e, cc=cc, i=i, lt2=lt2, pt=pt: e.transpose(
                                pt[:, i, :], lt2[:, cc * P:(cc + 1) * P], ident_t[:]),
                                reads=[lb2, ident_b], writes=[pb])
                        p.step("vector", lambda e, g=g, ft=ft, pt=pt: e.tensor_copy(
                            ft[:, g * 4:(g + 1) * 4, :], pt[:]),
                            reads=[pb], writes=[fb])
                for slab in range(6 if A_QKV else 0):
                    pg = slab % 2
                    def mm_slab(e, slab=slab, ft=ft, pg=pg):
                        for kc in range(NKC):
                            i = e.matmul(pp[pg][:], ft[:, kc, :],
                                         wqkv_t[:, kc, slab * 512:(slab + 1) * 512],
                                         start=(kc == 0), stop=(kc == NKC - 1))
                        return i
                    p.step("tensor", mm_slab, reads=[fb, wqkv_b], writes=[pp_b[pg]])
                    if slab < 2:
                        p.step("scalar", lambda e, slab=slab, cs=cs, d=d, pg=pg: e.activation(
                            out=q_tm_t[d][:, slab * 512:(slab + 1) * 512],
                            in_=pp[pg][:], func=AF.Copy,
                            scale=invt_t[:, cs:cs + 1]),
                            reads=[pp_b[pg], invt_b], writes=[q_tm_b[d]])
                    elif slab < 4:
                        p.step("scalar", lambda e, slab=slab, d=d, pg=pg: e.activation(
                            out=k_tm_t[d][:, (slab - 2) * 512:(slab - 1) * 512],
                            in_=pp[pg][:], func=AF.Sign),
                            reads=[pp_b[pg]], writes=[k_tm_b[d]])
                    else:
                        p.step("scalar", lambda e, slab=slab, d=d, pg=pg: e.activation(
                            out=v_tm_t[d][:, (slab - 4) * 512:(slab - 3) * 512],
                            in_=pp[pg][:], func=AF.Sign),
                            reads=[pp_b[pg]], writes=[v_tm_b[d]])

            def a2(ct):
                cs = ct % NCH_SEQ
                xb, xt = x_b[ct % 3], x_t[ct % 3]
                d = ct % 2
                qfm, qfmb = q_fm_t[d], q_fm_b[d]
                kfm, kfmb = k_fm_t[d], k_fm_b[d]
                ktm, ktmb = k_tm_t[d], k_tm_b[d]
                vtm, vtmb = v_tm_t[d], v_tm_b[d]
                if cs == 0 and A_ATT:
                    p.step("gpsimd", lambda e: e.memset(state_t[:], 0.0),
                           writes=[state_b])
                    p.step("gpsimd", lambda e: e.memset(stateb_t[:], 0.0),
                           writes=[stateb_b])
                scp = [(ps_scA, ps_scB, [ps_scA_b, ps_scB_b]),
                       (pp[0][:, 0:P], pp[1][:, 0:P], [pp_b[0], pp_b[1]])]

                def emit_sc(hp):
                    par = hp % 2
                    pa, pb, bufs = scp[par]
                    def mm_sc(e, hp=hp, pa=pa, pb=pb):
                        e.matmul(pa[:], kfm[0:64, hp, :], qfm[0:64, hp, :],
                                 start=True, stop=True)
                        return e.matmul(pb[:], kfm[64:128, hp, :],
                                        qfm[64:128, hp, :], start=True, stop=True)
                    p.step("tensor", mm_sc, reads=[kfmb, qfmb], writes=bufs)

                def emit_mask(hp):
                    par = hp % 2
                    pa, pb, bufs = scp[par]
                    p.step("vector", lambda e, par=par, pa=pa: e.tensor_tensor(
                        out=scT_t[:, 2 * par, :], in0=pa[:], in1=maskT_t[:],
                        op=ALU.mult),
                        reads=[bufs[0], maskT_b], writes=[scT_b[par]])
                    p.step("vector", lambda e, par=par, pb=pb: e.tensor_tensor(
                        out=scT_t[:, 2 * par + 1, :], in0=pb[:], in1=maskT_t[:],
                        op=ALU.mult),
                        reads=[bufs[1], maskT_b], writes=[scT_b[par]])

                def emit_att(hp):
                    par = hp % 2
                    def mm_att(e, hp=hp, par=par):
                        e.matmul(ps_o[0:64, hp % 4, :], vtm[:, hp * P:hp * P + 64],
                                 scT_t[:, 2 * par, :], start=True, stop=False)
                        e.matmul(ps_o[0:64, hp % 4, :], stateb_t[0:64, hp * 64:(hp + 1) * 64],
                                 qfm[0:64, hp, :], start=False, stop=True)
                        e.matmul(ps_o[64:128, hp % 4, :], vtm[:, hp * P + 64:hp * P + 128],
                                 scT_t[:, 2 * par + 1, :], start=True, stop=False)
                        return e.matmul(ps_o[64:128, hp % 4, :], stateb_t[64:128, hp * 64:(hp + 1) * 64],
                                        qfm[64:128, hp, :], start=False, stop=True)
                    p.step("tensor", mm_att, reads=[vtmb, scT_b[par], stateb_b, qfmb],
                           writes=[ps_o_b[hp % 4]])

                def emit_ckv(hp):
                    def mm_ckv(e, hp=hp):
                        e.matmul(ps_kv[0:64, hp * 64:(hp + 1) * 64],
                                 ktm[:, hp * P:hp * P + 64],
                                 vtm[:, hp * P:hp * P + 64], start=True, stop=True)
                        return e.matmul(ps_kv[64:128, hp * 64:(hp + 1) * 64],
                                        ktm[:, hp * P + 64:hp * P + 128],
                                        vtm[:, hp * P + 64:hp * P + 128],
                                        start=True, stop=True)
                    p.step("tensor", mm_ckv, reads=[ktmb, vtmb], writes=[ps_kv_b])

                if A_ATT:
                    emit_sc(0)
                    for hp in range(NHP):
                        emit_mask(hp)
                        if hp + 1 < NHP:
                            emit_sc(hp + 1)
                        emit_att(hp)
                        emit_ckv(hp)
                        if hp % 4 == 3:
                            g = hp // 4
                            p.step("scalar", lambda e, g=g: e.copy(
                                out=ao_t[:, g * 4:(g + 1) * 4, :], in_=ps_o[:, 0:4, :]),
                                reads=[ps_o_b[0], ps_o_b[1], ps_o_b[2], ps_o_b[3]],
                                writes=[ao_b])
                    p.step("vector", lambda e: e.tensor_tensor(
                        out=state_t[:], in0=state_t[:], in1=ps_kv[:], op=ALU.add),
                        reads=[ps_kv_b, state_b], writes=[state_b])
                    p.step("vector", lambda e: e.tensor_copy(stateb_t[:], state_t[:]),
                           reads=[state_b], writes=[stateb_b])
                x1b, x1t = x1_b[ct % 2], x1_t[ct % 2]
                for slab in range(2 if A_WO else 0):
                    pg = slab % 2
                    def mm_wo(e, slab=slab, pg=pg):
                        for kc in range(NKC):
                            i = e.matmul(pp[pg][:], ao_t[:, kc, :],
                                         wo_t[:, kc, slab * 512:(slab + 1) * 512],
                                         start=(kc == 0), stop=(kc == NKC - 1))
                        return i
                    p.step("tensor", mm_wo, reads=[ao_b, wo_b], writes=[pp_b[pg]])
                    p.step("vector", lambda e, slab=slab, xt=xt, x1t=x1t, pg=pg: e.tensor_tensor(
                        out=x1t[:, slab * 512:(slab + 1) * 512], in0=pp[pg][:],
                        in1=xt[:, slab * 512:(slab + 1) * 512], op=ALU.add),
                        reads=[pp_b[pg], xb], writes=[x1b])
                p.step("sync", lambda e, ct=ct, x1t=x1t: e.dma_start(
                    out=x1_d[ct * P:(ct + 1) * P, :], in_=x1t[:]),
                    reads=[x1b], writes=[x1d_bufs[ct]], dma=True)

            if A_ON:
                a_pre(0)
                a_pre(1)
                a_prep(0)
                a_pre(2)
                a_prep(1)
                a1(0)
            pu = 0
            pending_t = []
            for ct in range(NCH):
                if PREP_ON:
                    while len(pending_t) > 1:
                        emit_prep_t_byidx(pending_t.pop(0))
                if A_ON:
                    if ct + 1 < NCH:
                        a1(ct + 1)
                    a2(ct)
                    if ct + 2 < NCH:
                        a_prep(ct + 2)
                    if ct + 3 < NCH:
                        a_pre(ct + 3)
                if PREP_ON:
                    while pu < len(prep_units) and pu < (ct + 1) * len(prep_units) // (NCH - 4):
                        while len(pending_t) >= 2:
                            emit_prep_t_byidx(pending_t.pop(0))
                        emit_prep_q(prep_units[pu], pu % 2)
                        pending_t.append(pu)
                        pu += 1
            if PREP_ON:
                while pu < len(prep_units):
                    while pending_t:
                        emit_prep_t_byidx(pending_t.pop(0))
                    emit_prep_q(prep_units[pu], pu % 2)
                    pending_t.append(pu)
                    pu += 1
                while pending_t:
                    emit_prep_t_byidx(pending_t.pop(0))

            with nc.Block() as blockA:
                p.flush(blockA)

        with ExitStack() as stB:
            sbB = lambda n, s, d=F32: stB.enter_context(nc.sbuf_tensor(n, s, d))
            psB = lambda n, s, d=F32: stB.enter_context(nc.psum_tensor(n, s, d))

            qB = QuantPair(nc, sbB, 704, cst_t, cst_b)

            wgq_t = sbB("wgqb", [P, NKC, DI], BF16)
            wuq_t = sbB("wuqb", [P, NKC, DI], BF16)
            wdq_t = sbB("wdqb", [P, NKI, DM], BF16)
            wgq_bs = [Buf() for _ in range(NKC)]
            wuq_bs = [Buf() for _ in range(NKC)]
            wdq_bs = [Buf() for _ in range(NKI)]
            for kc in range(NKC if B_ON else 0):
                p.step("sync", lambda e, kc=kc: e.dma_start(
                    out=wgq_t[:, kc, :], in_=wgq_d[kc * P:(kc + 1) * P, :]),
                    reads=[wgq_db], writes=[wgq_bs[kc]], dma=True)
                p.step("sync", lambda e, kc=kc: e.dma_start(
                    out=wuq_t[:, kc, :], in_=wuq_d[kc * P:(kc + 1) * P, :]),
                    reads=[wuq_db], writes=[wuq_bs[kc]], dma=True)
            for kc in range(NKI if B_ON else 0):
                p.step("sync", lambda e, kc=kc: e.dma_start(
                    out=wdq_t[:, kc, :], in_=wdq_d[kc * P:(kc + 1) * P, :]),
                    reads=[wdq_db], writes=[wdq_bs[kc]], dma=True)

            x1b_t = [sbB("x1in%d" % i, [P, DM]) for i in range(3)]
            x1b_b = [Buf() for _ in range(3)]
            ln2_t = sbB("ln2", [P, DM]); ln2_b = Buf()
            xq_t = [sbB("xq%d" % i, [P, DM], BF16) for i in range(2)]
            xq_b = [Buf(), Buf()]
            xq_fm_t = [sbB("xqfm0", [P, NKC, P], BF16)] * 2
            xq_fm_b0 = Buf()
            xq_fm_b = [xq_fm_b0, xq_fm_b0]
            h_t0 = sbB("hhalf", [P, HI])
            h_b0 = Buf()
            h_t = [h_t0, h_t0]
            h_b = [h_b0, h_b0]
            hq_t = [[sbB("hq%d_%d" % (j, i), [P, HI], BF16) for i in range(2)]
                    for j in range(2)]
            hq_b = [[Buf(), Buf()], [Buf(), Buf()]]
            hq_fm_t = [sbB("hqfm%d" % i, [P, NKI, P], BF16) for i in range(2)]
            hq_fm_b = [Buf(), Buf()]
            ps_g = [psB("ps_g%d" % i, [P, 512]) for i in range(2)]
            ps_g_b = [Buf(), Buf()]
            ps_u = [psB("ps_u%d" % i, [P, 512]) for i in range(2)]
            ps_u_b = [Buf(), Buf()]
            ps_d = [psB("ps_d%d" % i, [P, 512]) for i in range(2)]
            ps_d_b = [Buf(), Buf()]
            ps_t2 = psB("ps_tr2", [P, 4, P], BF16); ps_t2_b = Buf()
            ps_t2B = psB("ps_tr2B", [P, 4, P], BF16); ps_t2B_b = Buf()
            pst2 = [ps_t2, ps_t2B]; pst2b = [ps_t2_b, ps_t2B_b]

            GSLABS = [(0, 512), (512, 512), (1024, 384)]

            out_toks = []

            def b_pre(ct):
                xb, xt = x1b_b[ct % 3], x1b_t[ct % 3]
                p.step("sync", lambda e, ct=ct, xt=xt: e.dma_start(
                    out=xt[:], in_=x1_d[ct * P:(ct + 1) * P, :]),
                    reads=[x1d_bufs[ct]], writes=[xb], dma=True)

            def b_prep(ct):
                xb, xt = x1b_b[ct % 3], x1b_t[ct % 3]
                emit_ln(p, xt, xb, ln2_t, ln2_b, ln)
                xqb, xqt = xq_b[ct % 2], xq_t[ct % 2]
                qB.emit_multi(p, ln2_t, ln2_b, xqt, xqb, DM,
                              z2_eng="vector", dst_eng="vector")

            def b1t(ct):
                fb, ft = xq_fm_b[ct % 2], xq_fm_t[ct % 2]
                xqb, xqt = xq_b[ct % 2], xq_t[ct % 2]
                for g in range(2):
                    pt, pb = pst2[g % 2], pst2b[g % 2]
                    for i in range(4):
                        cc = g * 4 + i
                        p.step("tensor", lambda e, cc=cc, i=i, xqt=xqt, pt=pt: e.transpose(
                            pt[:, i, :], xqt[:, cc * P:(cc + 1) * P], ident_t[:]),
                            reads=[xqb, ident_b], writes=[pb])
                    p.step("scalar", lambda e, g=g, ft=ft, pt=pt: e.copy(
                        out=ft[:, g * 4:(g + 1) * 4, :], in_=pt[:]),
                        reads=[pb], writes=[fb])

            def b1(ct):
                fb, ft = xq_fm_b[ct % 2], xq_fm_t[ct % 2]
                d = ct % 2
                hfmt, hfmb = hq_fm_t[d], hq_fm_b[d]
                for half in range(2):
                    hb, ht = h_b[half], h_t[half]
                    hqb, hqt = hq_b[d][half], hq_t[d][half]
                    for si, (off, wl) in enumerate(GSLABS):
                        pg = si % 2

                        def mm_g(e, half=half, ft=ft, off=off, wl=wl, pg=pg):
                            for kc in range(NKC):
                                i = e.matmul(ps_g[pg][:, 0:wl], ft[:, kc, :],
                                             wgq_t[:, kc, half * HI + off:half * HI + off + wl],
                                             start=(kc == 0), stop=(kc == NKC - 1))
                            return i
                        p.step("tensor", mm_g, reads=[fb] + wgq_bs, writes=[ps_g_b[pg]])

                        def mm_u(e, half=half, ft=ft, off=off, wl=wl, pg=pg):
                            for kc in range(NKC):
                                i = e.matmul(ps_u[pg][:, 0:wl], ft[:, kc, :],
                                             wuq_t[:, kc, half * HI + off:half * HI + off + wl],
                                             start=(kc == 0), stop=(kc == NKC - 1))
                            return i
                        p.step("tensor", mm_u, reads=[fb] + wuq_bs, writes=[ps_u_b[pg]])
                        p.step("scalar", lambda e, ht=ht, off=off, wl=wl, pg=pg: e.activation(
                            out=ht[:, off:off + wl], in_=ps_g[pg][:, 0:wl], func=AF.Silu),
                            reads=[ps_g_b[pg]], writes=[hb])
                        p.step("vector", lambda e, ht=ht, off=off, wl=wl, pg=pg: e.tensor_tensor(
                            out=ht[:, off:off + wl], in0=ps_u[pg][:, 0:wl],
                            in1=ht[:, off:off + wl], op=ALU.mult),
                            reads=[ps_u_b[pg], hb], writes=[hb])
                    qB.emit_multi(p, ht, hb, hqt, hqb, HI,
                                  z2_eng="gpsimd", dst_eng="gpsimd")

            def b2(ct):
                xb, xt = x1b_b[ct % 3], x1b_t[ct % 3]
                d = ct % 2
                hfmt, hfmb = hq_fm_t[d], hq_fm_b[d]
                bi = 0
                for half in range(2):
                    hqb, hqt = hq_b[d][half], hq_t[d][half]
                    ccs = list(range(11))
                    for g in range(3):
                        grp = ccs[g * 4:(g + 1) * 4]
                        if not grp:
                            break
                        pt, pb = pst2[bi % 2], pst2b[bi % 2]
                        bi += 1
                        for i, cc in enumerate(grp):
                            p.step("tensor", lambda e, cc=cc, i=i, hqt=hqt, pt=pt: e.transpose(
                                pt[:, i, :], hqt[:, cc * P:(cc + 1) * P], ident_t[:]),
                                reads=[hqb, ident_b], writes=[pb])
                        p.step("scalar", lambda e, g=g, grp=grp, half=half, hfmt=hfmt, pt=pt: e.copy(
                            out=hfmt[:, half * 11 + grp[0]:half * 11 + grp[0] + len(grp), :],
                            in_=pt[:, 0:len(grp), :]),
                            reads=[pb], writes=[hfmb])
                for slab in range(2):
                    pg = slab % 2
                    def mm_d(e, slab=slab, hfmt=hfmt, pg=pg):
                        for kc in range(NKI):
                            i = e.matmul(ps_d[pg][:], hfmt[:, kc, :],
                                         wdq_t[:, kc, slab * 512:(slab + 1) * 512],
                                         start=(kc == 0), stop=(kc == NKI - 1))
                        return i
                    p.step("tensor", mm_d, reads=[hfmb] + wdq_bs, writes=[ps_d_b[pg]])
                    p.step("vector", lambda e, slab=slab, xt=xt, pg=pg: e.tensor_tensor(
                        out=xt[:, slab * 512:(slab + 1) * 512], in0=ps_d[pg][:],
                        in1=xt[:, slab * 512:(slab + 1) * 512], op=ALU.add),
                        reads=[ps_d_b[pg], xb], writes=[xb])
                out_toks.append(p.step("sync", lambda e, ct=ct, xt=xt: e.dma_start(
                    out=out_h[ct * P:(ct + 1) * P, :], in_=xt[:]),
                    reads=[xb], dma=True))

            if B_ON:
                b_pre(0)
                b_pre(1)
                b_prep(0)
                b_pre(2)
                b_prep(1)
                b1t(0)
                b1(0)
                for ct in range(NCH):
                    if ct + 1 < NCH:
                        b1t(ct + 1)
                        b1(ct + 1)
                    if ct + 2 < NCH:
                        b_prep(ct + 2)
                    b2(ct)
                    if ct + 3 < NCH:
                        b_pre(ct + 3)

            p.step("gpsimd", lambda e: e.memset(ln["st_t"][:, 0:1], 0.0),
                   extra=out_toks, writes=[ln["st"]])

            with nc.Block() as blockB:
                p.flush(blockB)
    return nc


_NC_CACHE = {}


def kernel(L0_attn_x, L0_attn_W_qkv, L0_attn_W_o, L0_ffn_w_gate, L0_ffn_w_up,
           L0_ffn_w_down):
    x = np.ascontiguousarray(np.asarray(L0_attn_x, dtype=np.float32))
    wqkvT = np.ascontiguousarray(np.asarray(L0_attn_W_qkv, dtype=np.float32).T)
    woT = np.ascontiguousarray(np.asarray(L0_attn_W_o, dtype=np.float32).T)
    wg = np.ascontiguousarray(np.asarray(L0_ffn_w_gate, dtype=np.float32))
    wu = np.ascontiguousarray(np.asarray(L0_ffn_w_up, dtype=np.float32))
    wd = np.ascontiguousarray(np.asarray(L0_ffn_w_down, dtype=np.float32))
    ident = np.eye(P, dtype=np.float32).astype(ml_dtypes.bfloat16)
    maskT = np.triu(np.ones((P, P), dtype=np.float32)).astype(ml_dtypes.bfloat16)
    tot = (np.arange(1, T + 1, dtype=np.float32).reshape(NCH_SEQ, P).T).copy()

    if "nc" not in _NC_CACHE:
        _NC_CACHE["nc"] = build_program()
    nc = _NC_CACHE["nc"]

    in_maps = []
    for c in range(NCORES):
        xs = np.ascontiguousarray(
            x[c * SEQ_PER_CORE:(c + 1) * SEQ_PER_CORE].reshape(TOK, DM))
        in_maps.append({"x": xs, "wqkvT": wqkvT, "woT": woT, "wg": wg,
                        "wu": wu, "wd": wd, "ident": ident, "maskT": maskT,
                        "tot": tot})
    res = run_bass_kernel_spmd(nc, in_maps, core_ids=list(range(NCORES)))
    out = np.concatenate([np.asarray(r["out"]) for r in res.results], axis=0)
    return out.reshape(B_FULL, T, DM).astype(np.float32)
